# version 09 (final)
# speedup vs baseline: 1.1234x; 1.0108x over previous
import numpy as np
import ml_dtypes
from contextlib import ExitStack
import concourse.bass as bass
import concourse.mybir as mybir
from concourse.bass_utils import run_bass_kernel_spmd
from concourse.bass_core_rust import IndirectOffsetOnAxis

F32 = mybir.dt.float32
BF16 = mybir.dt.bfloat16
I32 = mybir.dt.int32
AF = mybir.ActivationFunctionType
ALU = mybir.AluOpType
AX = mybir.AxisListType
BF = ml_dtypes.bfloat16

NCORE = 8
B, T, D = 2, 8192, 1024
TLOC = T * B // NCORE
GROUPS = [[0, 1, 2, 3], [4, 5, 6, 7]]

MAGIC_ROUND = 12582912.0


class Prog:
    ENGINES = ("sync", "tensor", "vector", "scalar", "gpsimd")

    def __init__(self, name, tloc=TLOC):
        self.name = name
        self.tloc = tloc
        self.nt = tloc // 128
        self.nc = bass.Bass("TRN2", target_bir_lowering=False)
        self.ctx = ExitStack()
        self.streams = {e: [] for e in self.ENGINES}
        self.sems = {}
        self.cnt = {}
        self.t = {}

    def dram(self, name, shape, dt=F32, kind="Internal"):
        h = self.nc.dram_tensor(name, shape, dt, kind=kind)
        self.t[name] = h
        return h

    def sbuf(self, name, shape, dt=F32):
        h = self.ctx.enter_context(self.nc.sbuf_tensor(name, shape, dt))
        self.t[name] = h
        return h

    def psum(self, name, shape, dt=F32):
        h = self.ctx.enter_context(self.nc.psum_tensor(name, shape, dt))
        self.t[name] = h
        return h

    def sem(self, name):
        s = self.ctx.enter_context(self.nc.semaphore(name))
        self.sems[name] = s
        self.cnt[name] = 0
        return s

    def on(self, eng, fn):
        self.streams[eng].append(fn)

    def wait(self, eng, sem_name, val=None):
        v = self.cnt[sem_name] if val is None else val
        s = self.sems[sem_name]
        self.on(eng, lambda e, s=s, v=v: e.wait_ge(s, v))

    ENG_SEM = {"tensor": "pe", "vector": "dve", "scalar": "act", "gpsimd": "pool"}

    def step(self, eng, sem_name, fn, by=1, chain=True):
        own = self.ENG_SEM.get(eng)
        if chain and own is not None and own in self.cnt and self.cnt[own] > 0:
            self.wait(eng, own)
        self.cnt[sem_name] += by
        s = self.sems[sem_name]
        self.on(eng, lambda e, fn=fn, s=s, by=by: fn(e).then_inc(s, by))
        return self.cnt[sem_name]

    def raw(self, eng, fn):
        self.on(eng, fn)

    def dma(self, queue, sem_name, out_ap, in_ap):
        return self.step(
            queue, sem_name,
            lambda e, o=out_ap, i=in_ap: e.dma_start(out=o, in_=i), by=16)

    def build(self):
        with self.ctx:
            with self.nc.Block() as block:
                for eng in self.ENGINES:
                    fns = self.streams[eng]
                    if fns:
                        getattr(block, eng)(
                            lambda e, fns=fns: [f(e) for f in fns])
        return self.nc

    ARENA_BASE = 4096
    ARENA_TOP = 228 * 1024

    def arena_reset(self):
        self.arena_off = self.ARENA_BASE
        self.psum_idx = 0

    def arena_sbuf(self, name, shape, dt=F32):
        per = 1
        for s in shape[1:]:
            per *= s
        per = (per * mybir.dt.size(dt) + 31) // 32 * 32
        if not hasattr(self, "arena_off"):
            self.arena_reset()
        assert self.arena_off + per <= self.ARENA_TOP, \
            (name, self.arena_off, per)
        h = self.nc.alloc_sbuf_tensor_at(name, shape, dt, offset=self.arena_off)
        self.arena_off += per
        return h

    def arena_psum(self, name, shape, dt=F32):
        if not hasattr(self, "psum_banks"):
            self.psum_banks = [self.ctx.enter_context(
                self.nc.psum_tensor(f"abank{i}", [128, 512], F32))
                for i in range(8)]
        per = 1
        for s in shape[1:]:
            per *= s
        assert per * mybir.dt.size(dt) <= 2048, (name, shape)
        assert self.psum_idx < 8, name
        v = self.nc.virtual_psum_tensor(name, shape, dt)
        self.nc.back_virtual_tensor_with(v, self.psum_banks[self.psum_idx])
        self.psum_idx += 1
        return v

    def barrier(self):
        for eng in self.ENGINES:
            for s in list(self.sems):
                self.wait(eng, s)


class SegmentProg(Prog):

    def __init__(self, parent, tag, out_kind="Internal", dram_alias=None):
        self.parent = parent
        self.name = tag
        self.tloc = parent.tloc
        self.nt = parent.nt
        self.nc = parent.nc
        self.ctx = parent.ctx
        self.streams = parent.streams
        self.sems = parent.sems
        self.cnt = parent.cnt
        self.t = {}
        self.out_kind = out_kind
        self.dram_alias = dram_alias or {}
        self.seg_ctx = ExitStack()

    def sbuf(self, name, shape, dt=F32):
        h = self.parent.arena_sbuf(f"{self.name}_{name}", shape, dt)
        self.t[name] = h
        return h

    def psum(self, name, shape, dt=F32):
        h = self.seg_ctx.enter_context(
            self.nc.psum_tensor(f"{self.name}_{name}", shape, dt))
        self.t[name] = h
        return h

    def dram(self, name, shape, dt=F32, kind="Internal"):
        if name in self.dram_alias:
            h = self.dram_alias[name]
        else:
            if name == "x_out":
                kind = self.out_kind
            h = self.nc.dram_tensor(f"{self.name}_{name}", shape, dt, kind=kind)
        self.t[name] = h
        return h

    def sem(self, name):
        if name in self.sems:
            return self.sems[name]
        return Prog.sem(self, name)

    def on(self, eng, fn):
        tag, nc_ = self.name, self.nc

        def wrapped(e, fn=fn):
            with nc_.named_scope(tag):
                return fn(e)
        self.streams[eng].append(wrapped)


def emit_consts(p):
    p.sbuf("c_eps5", [128, 1]); p.sbuf("c_eps6", [128, 1])
    p.sbuf("c_inv1024", [128, 1]); p.sbuf("c_neghalf", [128, 1])
    p.sbuf("ones_row", [1, 128])
    for nm, v in [("c_eps5", 1e-5), ("c_eps6", 1e-6),
                  ("c_inv1024", 1.0 / 1024.0), ("c_neghalf", -0.5),
                  ("ones_row", 1.0)]:
        p.step("vector", "dve",
               lambda e, t=p.t[nm], v=v: e.memset(t[:, :], v))


def emit_ln_xnt(p, x_name="x_s", xnt_name="xnT", ident_name="ident_s",
                sem_ready="dve", extra_wait=None, per_tile_hook=None, out32=False):
    nt = p.nt
    x_s = p.t[x_name]
    xnT = p.t[xnt_name]
    ident = p.t[ident_name]
    xn_bf = p.t["xn_bf"]
    sq = p.t["sq_scratch"]
    mcol = p.t["m_col"]; ssq = p.t["ssq_col"]; rstd = p.t["rstd_col"]
    nbias = p.t["nb_col"]
    pT = [p.t["pT0"], p.t["pT1"]]

    for i in range(nt):
        p.wait("vector", "act")
        v_sum = p.step("vector", "dve",
                       lambda e, i=i: e.reduce_sum(mcol[:, :], x_s[:, i, :], axis=AX.X))
        v_neg = p.step("vector", "dve",
                       lambda e: e.tensor_scalar(mcol[:, :], mcol[:, :], -1.0 / 1024.0, None, ALU.mult))
        p.wait("scalar", "dve", v_neg)
        a_sq = p.step("scalar", "act",
                      lambda e, i=i: e.activation(sq[:, :], x_s[:, i, :], AF.Square,
                                                  bias=mcol[:, :], accum_out=ssq[:, :]))
        a_ln = p.step("scalar", "act",
                      lambda e: e.activation(rstd[:, :], ssq[:, :], AF.Ln,
                                             bias=p.t["c_eps5"][:, :], scale=p.t["c_inv1024"][:, :]))
        a_ex = p.step("scalar", "act",
                      lambda e: e.activation(rstd[:, :], rstd[:, :], AF.Exp,
                                             scale=p.t["c_neghalf"][:, :]))
        p.wait("vector", "act", a_ex)
        v_nb = p.step("vector", "dve",
                      lambda e: e.tensor_mul(nbias[:, :], mcol[:, :], rstd[:, :]))
        a_xn = p.step("vector", "dve",
                      lambda e, i=i: e.tensor_scalar(xn_bf[:, :], x_s[:, i, :],
                                                     rstd[:, :], nbias[:, :],
                                                     ALU.mult, ALU.add))
        if out32:
            p.step("vector", "dve",
                   lambda e, i=i: e.tensor_scalar(p.t["xn_32"][:, :], x_s[:, i, :],
                                                  rstd[:, :], nbias[:, :],
                                                  ALU.mult, ALU.add))
        p.wait("tensor", "dve", a_xn)
        for h in range(2):
            p.wait("tensor", "act")
            for jj in range(4):
                j = 4 * h + jj
                last = (jj == 3)
                if last:
                    p.step("tensor", "pe",
                           lambda e, h=h, j=j, jj=jj: e.transpose(
                               pT[h][:, 128 * jj:128 * (jj + 1)],
                               xn_bf[:, 128 * j:128 * (j + 1)], ident[:, :]))
                else:
                    p.raw("tensor",
                          lambda e, h=h, j=j, jj=jj: e.transpose(
                              pT[h][:, 128 * jj:128 * (jj + 1)],
                              xn_bf[:, 128 * j:128 * (j + 1)], ident[:, :]))
            p.wait("scalar", "pe")
            p.step("scalar", "act",
                   lambda e, i=i, h=h: e.activation(
                       xnT[:, 4 * h:4 * h + 4, 128 * i:128 * (i + 1)],
                       pT[h][:, :].rearrange("p (a b) -> p a b", a=4), AF.Copy))
        if per_tile_hook is not None:
            per_tile_hook(p, i)


def emit_dense(p, xnt_name, w_name, bias_row=None, nj=8, evac_fn=None,
               psum_names=("pP0", "pP1"), pe_guard="dve", nhalf=2, wait_act=True):
    nt = p.nt
    xnT = p.t[xnt_name]
    w = p.t[w_name]
    pP = [p.t[n] for n in psum_names]
    if wait_act:
        p.wait("tensor", "act")
        p.wait("tensor", "dve")
    for i in range(nt):
        for half in range(nhalf):
            pp = pP[(nhalf * i + half) % len(pP)]
            p.wait("tensor", pe_guard)
            has_bias = bias_row is not None
            for j in range(nj):
                last = (j == nj - 1) and not has_bias
                fn = lambda e, i=i, half=half, j=j, pp=pp, last=last: e.matmul(
                    pp[:, :], xnT[:, j, 128 * i:128 * (i + 1)],
                    w[:, j, 512 * half:512 * (half + 1)],
                    start=(j == 0), stop=last)
                if last:
                    p.step("tensor", "pe", fn, chain=False)
                else:
                    p.raw("tensor", fn)
            if has_bias:
                b = p.t[bias_row]
                p.step("tensor", "pe",
                       lambda e, half=half, pp=pp, b=b: e.matmul(
                           pp[:, :], p.t["ones_row"][0:1, :],
                           b[0:1, 512 * half:512 * (half + 1)],
                           start=False, stop=True), chain=False)
            evac_fn(p, i, half, pp)


def evac_residual(x_name="x_s"):
    def fn(p, i, half, pp):
        x_s = p.t[x_name]
        p.wait("vector", "pe")
        p.step("vector", "dve",
               lambda e, i=i, half=half, pp=pp: e.tensor_add(
                   x_s[:, i, 512 * half:512 * (half + 1)], pp[:, :],
                   x_s[:, i, 512 * half:512 * (half + 1)]))
    return fn


def build_ffn(tloc=TLOC, p=None):
    p = p or Prog("ffn", tloc)
    nt = p.nt
    x_in = p.dram("x_in", [tloc, 1024], F32, kind="ExternalInput")
    p.dram("wT", [1024, 1024], F32, kind="ExternalInput")
    p.dram("b_in", [1, 1024], F32, kind="ExternalInput")
    p.dram("ldT", [100, 4], F32, kind="ExternalInput")
    p.dram("down_aux", [100, 1024], F32, kind="ExternalInput")
    p.dram("luT", [4, 50], F32, kind="ExternalInput")
    p.dram("up_auxT", [50, 1024], F32, kind="ExternalInput")
    p.dram("ident_in", [128, 128], BF16, kind="ExternalInput")
    x_out = p.dram("x_out", [tloc, 1024], F32, kind="ExternalOutput")

    p.sbuf("x_s", [128, nt, 1024])
    p.sbuf("wT_bf", [128, 8, 1024], BF16)
    p.sbuf("weff", [128, 8, 1024], BF16)
    p.sbuf("xnT", [128, 8, tloc], BF16)
    p.sbuf("xn_bf", [128, 1024], BF16)
    p.sbuf("sq_scratch", [128, 1024])
    p.sbuf("ident_s", [128, 128], BF16)
    p.sbuf("b_s", [1, 1024])
    p.sbuf("ldT_s", [100, 4]); p.sbuf("down_s", [100, 1024])
    p.sbuf("luT_s", [4, 50]); p.sbuf("upT_s", [50, 1024])
    p.sbuf("a1_s", [4, 1024]); p.sbuf("a2_s", [50, 1024])
    for nm in ["m_col", "ssq_col", "rstd_col", "nb_col"]:
        p.sbuf(nm, [128, 1])
    p.psum("pT0", [128, 512], BF16); p.psum("pT1", [128, 512], BF16)
    p.psum("pP0", [128, 512]); p.psum("pP1", [128, 512])
    p.psum("pW", [128, 1024])
    p.psum("pA", [64, 1024])

    for s in ["dx", "dw", "dm", "pe", "dve", "act", "st"]:
        p.sem(s)

    emit_consts(p)

    p.dma("sync", "dx", p.t["x_s"][:, :, :],
          x_in.ap().rearrange("(i p) d -> p i d", p=128))
    p.dma("gpsimd", "dw", p.t["wT_bf"][:, :, :],
          p.t["wT"].ap().rearrange("(j p) e -> p j e", p=128))
    for nm, src in [("ident_s", "ident_in"), ("b_s", "b_in"), ("ldT_s", "ldT"),
                    ("down_s", "down_aux"), ("luT_s", "luT"), ("upT_s", "up_auxT")]:
        p.dma("sync", "dm", p.t[nm][:, :], p.t[src].ap())

    p.wait("tensor", "dm")
    p.step("tensor", "pe", lambda e: e.matmul(p.t["pA"][0:4, 0:512], p.t["ldT_s"][:, :],
                                              p.t["down_s"][:, 0:512], start=True, stop=True))
    p.step("tensor", "pe", lambda e: e.matmul(p.t["pA"][0:4, 512:1024], p.t["ldT_s"][:, :],
                                              p.t["down_s"][:, 512:1024], start=True, stop=True))
    p.wait("vector", "pe")
    p.step("vector", "dve", lambda e: e.tensor_copy(p.t["a1_s"][:, :], p.t["pA"][0:4, :]))
    p.wait("tensor", "dve")
    p.step("tensor", "pe", lambda e: e.matmul(p.t["pA"][0:50, 0:512], p.t["luT_s"][:, :],
                                              p.t["a1_s"][:, 0:512], start=True, stop=True))
    p.step("tensor", "pe", lambda e: e.matmul(p.t["pA"][0:50, 512:1024], p.t["luT_s"][:, :],
                                              p.t["a1_s"][:, 512:1024], start=True, stop=True))
    p.wait("vector", "pe")
    p.step("vector", "dve", lambda e: e.tensor_copy(p.t["a2_s"][:, :], p.t["pA"][0:50, :]))
    p.wait("tensor", "dve")
    p.wait("tensor", "dw")
    for j in range(8):
        p.step("tensor", "pe",
               lambda e, j=j: e.matmul(p.t["pW"][:, 0:512], p.t["a2_s"][:, 128 * j:128 * (j + 1)],
                                       p.t["upT_s"][:, 0:512], start=True, stop=True))
        p.step("tensor", "pe",
               lambda e, j=j: e.matmul(p.t["pW"][:, 512:1024], p.t["a2_s"][:, 128 * j:128 * (j + 1)],
                                       p.t["upT_s"][:, 512:1024], start=True, stop=True))
        p.wait("vector", "pe")
        p.step("vector", "dve",
               lambda e, j=j: e.tensor_add(p.t["weff"][:, j, :], p.t["pW"][:, :],
                                           p.t["wT_bf"][:, j, :]))
        p.wait("tensor", "dve")

    p.wait("vector", "dx")
    p.wait("scalar", "dx")
    emit_ln_xnt(p)

    emit_dense(p, "xnT", "weff", bias_row="b_s", evac_fn=evac_residual())

    p.wait("sync", "dve")
    p.dma("sync", "st", x_out.ap().rearrange("(i p) d -> p i d", p=128),
          p.t["x_s"][:, :, :])
    p.wait("sync", "st")
    return p


def emit_allgather(p, tot_row_ap, ag_s_ap, dram_src, dram_dst, wait_sem, wait_val,
                   groups=None):
    groups = groups or p.groups
    p.wait("gpsimd", wait_sem, wait_val)
    p.dma("gpsimd", "dcc", dram_src.ap(), tot_row_ap)
    p.wait("gpsimd", "dcc")
    p.step("gpsimd", "cc",
           lambda e, groups=groups: e.collective_compute(
               "AllGather", ALU.bypass, replica_groups=groups,
               ins=[dram_src.ap().opt()], outs=[dram_dst.ap().opt()]))
    p.wait("gpsimd", "cc")
    p.dma("gpsimd", "dcc", ag_s_ap, dram_dst.ap())


def build_rwkv(tloc=TLOC, groups=None, p=None):
    p = p or Prog("rwkv", tloc)
    p.groups = groups or GROUPS
    gsize = len(p.groups[0])
    nt = p.nt
    x_in = p.dram("x_in", [tloc, 1024], F32, kind="ExternalInput")
    for nm in ["wrT", "wkT", "wvT", "woT"]:
        p.dram(nm, [1024, 1024], F32, kind="ExternalInput")
    p.dram("decay_in", [1, 1024], F32, kind="ExternalInput")
    p.dram("lnw_col_in", [128, 8], F32, kind="ExternalInput")
    p.dram("iota_in", [1, tloc], F32, kind="ExternalInput")
    p.dram("sel_in", [gsize, 1], F32, kind="ExternalInput")
    p.dram("triuT_in", [128, 128], F32, kind="ExternalInput")
    p.dram("ident_in", [128, 128], BF16, kind="ExternalInput")
    x_out = p.dram("x_out", [tloc, 1024], F32, kind="ExternalOutput")
    tot_d = p.dram("tot_d", [1, 1024], F32)
    ag_d = p.dram("ag_d", [gsize, 1024], F32)

    p.sbuf("x_t", [128, 2, 1024])
    p.sbuf("xnT", [128, 8, tloc], BF16)
    p.sbuf("xn_bf", [128, 1024], BF16)
    p.sbuf("sq_scratch", [128, 1024])
    for nm in ["wr_s", "wk_s", "wv_s", "wo_s"]:
        p.sbuf(nm, [128, 8, 1024], BF16)
    p.sbuf("dec_row", [1, 1024]); p.sbuf("ld_row", [1, 1024])
    p.sbuf("lnw_col", [128, 8])
    p.sbuf("iota_s", [1, tloc]); p.sbuf("sel_s", [gsize, 1])
    p.sbuf("triuT_s", [128, 128]); p.sbuf("ident_s", [128, 128], BF16)
    p.sbuf("tot_row", [1, 1024]); p.sbuf("pre_row", [1, 1024])
    p.sbuf("run_row", [1, 1024]); p.sbuf("ag_s", [gsize, 1024])
    p.sbuf("sK", [128, 512]); p.sbuf("skv", [128, 512])
    p.sbuf("srs", [128, 512]); p.sbuf("kvt", [128, 512])
    p.sbuf("ssc", [128, 512]); p.sbuf("str", [128, 512])
    p.sbuf("r_t", [128, 512])
    p.sbuf("y_t", [128, 1024])
    p.sbuf("yn_bf", [128, 1024], BF16)
    p.sbuf("ynT_t", [128, 8, 128], BF16)
    p.sbuf("ones_col", [128, 1]); p.sbuf("c_negone", [128, 1])
    for nm in ["m_col", "ssq_col", "rstd_col", "nb_col"]:
        p.sbuf(nm, [128, 1])
    p.psum("pT0", [128, 512], BF16); p.psum("pT1", [128, 512], BF16)
    p.psum("pA", [128, 512]); p.psum("pL", [128, 512])
    p.psum("pO", [128, 512])
    p.psum("pTot", [1, 1024])

    for s in ["dx", "dw", "dm", "pe", "dve", "act", "pool", "dcc", "cc", "st"]:
        p.sem(s)

    emit_consts(p)
    p.step("vector", "dve", lambda e: e.memset(p.t["ones_col"][:, :], 1.0))
    p.step("vector", "dve", lambda e: e.memset(p.t["run_row"][:, :], 0.0))
    p.step("vector", "dve", lambda e: e.memset(p.t["c_negone"][:, :], -1.0))

    for nm, src in [("dec_row", "decay_in"), ("lnw_col", "lnw_col_in"),
                    ("iota_s", "iota_in"), ("sel_s", "sel_in"),
                    ("triuT_s", "triuT_in"), ("ident_s", "ident_in")]:
        p.dma("sync", "dm", p.t[nm][:, :], p.t[src].ap())
    for nm, src in [("wr_s", "wrT"), ("wk_s", "wkT"), ("wv_s", "wvT"), ("wo_s", "woT")]:
        p.dma("gpsimd", "dw", p.t[nm][:, :, :],
              p.t[src].ap().rearrange("(j p) e -> p j e", p=128))

    p.wait("scalar", "dm")
    p.step("scalar", "act", lambda e: e.activation(p.t["dec_row"][:, :], p.t["dec_row"][:, :], AF.Sigmoid))
    p.wait("vector", "act")
    p.step("vector", "dve", lambda e: e.tensor_scalar_max(p.t["dec_row"][:, :], p.t["dec_row"][:, :], 1e-7))
    p.wait("scalar", "dve")
    p.step("scalar", "act", lambda e: e.activation(p.t["ld_row"][:, :], p.t["dec_row"][:, :], AF.Ln))
    p.wait("vector", "dw")
    for j in range(8):
        p.step("vector", "dve",
               lambda e, j=j: e.tensor_scalar(p.t["wo_s"][:, j, :], p.t["wo_s"][:, j, :],
                                              p.t["lnw_col"][:, j:j + 1], None, ALU.mult))

    x_t = p.t["x_t"]

    def ln_tile(i):
        p.wait("vector", "dx", 16 * (i + 1))
        mcol = p.t["m_col"]; ssq = p.t["ssq_col"]; rstd = p.t["rstd_col"]
        nbias = p.t["nb_col"]; sq = p.t["sq_scratch"]; xn_bf = p.t["xn_bf"]
        xnT = p.t["xnT"]; ident = p.t["ident_s"]
        pT = [p.t["pT0"], p.t["pT1"]]
        p.wait("vector", "act")
        p.step("vector", "dve", lambda e, i=i: e.reduce_sum(mcol[:, :], x_t[:, i % 2, :], axis=AX.X))
        p.step("vector", "dve", lambda e: e.tensor_scalar(mcol[:, :], mcol[:, :], -1.0 / 1024.0, None, ALU.mult))
        p.wait("scalar", "dve")
        p.step("scalar", "act", lambda e, i=i: e.activation(sq[:, :], x_t[:, i % 2, :], AF.Square,
                                                            bias=mcol[:, :], accum_out=ssq[:, :]))
        p.step("scalar", "act", lambda e: e.activation(rstd[:, :], ssq[:, :], AF.Ln,
                                                       bias=p.t["c_eps5"][:, :], scale=p.t["c_inv1024"][:, :]))
        p.step("scalar", "act", lambda e: e.activation(rstd[:, :], rstd[:, :], AF.Exp,
                                                       scale=p.t["c_neghalf"][:, :]))
        p.wait("vector", "act")
        p.step("vector", "dve", lambda e: e.tensor_mul(nbias[:, :], mcol[:, :], rstd[:, :]))
        p.step("vector", "dve", lambda e, i=i: e.tensor_scalar(xn_bf[:, :], x_t[:, i % 2, :],
                                                               rstd[:, :], nbias[:, :], ALU.mult, ALU.add))
        p.wait("tensor", "dve")
        for h in range(2):
            p.wait("tensor", "act")
            for jj in range(4):
                j = 4 * h + jj
                fn = lambda e, h=h, j=j, jj=jj: e.transpose(
                    pT[h][:, 128 * jj:128 * (jj + 1)],
                    xn_bf[:, 128 * j:128 * (j + 1)], ident[:, :])
                if jj == 3:
                    p.step("tensor", "pe", fn, chain=False)
                else:
                    p.raw("tensor", fn)
            p.wait("scalar", "pe")
            p.step("scalar", "act",
                   lambda e, i=i, h=h: e.activation(
                       xnT[:, 4 * h:4 * h + 4, 128 * i:128 * (i + 1)],
                       pT[h][:, :].rearrange("p (a b) -> p a b", a=4), AF.Copy))
    for i in range(nt):
        p.wait("sync", "dve")
        p.dma("sync", "dx", x_t[:, i % 2, :],
              x_in.ap().rearrange("(ii pp) d -> pp ii d", pp=128)[:, i, :])
        ln_tile(i)

    def kv_tile(i, half):
        xnT = p.t["xnT"]
        p.wait("tensor", "act")
        p.wait("tensor", "dve")
        for j in range(8):
            fn = lambda e, i=i, half=half, j=j: e.matmul(
                p.t["pA"][:, :], xnT[:, j, 128 * i:128 * (i + 1)],
                p.t["wk_s"][:, j, 512 * half:512 * (half + 1)],
                start=(j == 0), stop=(j == 7))
            if j == 7:
                p.step("tensor", "pe", fn, chain=False)
            else:
                p.raw("tensor", fn)
        p.wait("scalar", "pe")
        a_k = p.step("scalar", "act", lambda e: e.activation(p.t["sK"][:, :], p.t["pA"][:, :], AF.Copy))
        p.wait("tensor", "act", a_k)
        for j in range(8):
            fn = lambda e, i=i, half=half, j=j: e.matmul(
                p.t["pA"][:, :], xnT[:, j, 128 * i:128 * (i + 1)],
                p.t["wv_s"][:, j, 512 * half:512 * (half + 1)],
                start=(j == 0), stop=(j == 7))
            if j == 7:
                p.step("tensor", "pe", fn, chain=False)
            else:
                p.raw("tensor", fn)
        p.wait("tensor", "act")
        p.step("tensor", "pe",
               lambda e, i=i, half=half: e.matmul(
                   p.t["pL"][:, :], p.t["iota_s"][0:1, 128 * i:128 * (i + 1)],
                   p.t["ld_row"][0:1, 512 * half:512 * (half + 1)],
                   start=True, stop=True), chain=False)
        p.wait("vector", "pe")
        p.step("vector", "dve", lambda e: e.tensor_mul(p.t["skv"][:, :], p.t["sK"][:, :], p.t["pA"][:, :]))
        p.wait("vector", "pe")
        p.step("vector", "dve", lambda e: e.tensor_scalar(p.t["srs"][:, :], p.t["pL"][:, :],
                                                          -1.0, 30.0, ALU.mult, ALU.min))
        p.wait("scalar", "dve")
        a_rs = p.step("scalar", "act", lambda e: e.activation(p.t["srs"][:, :], p.t["srs"][:, :], AF.Exp))
        p.wait("vector", "act", a_rs)
        p.step("vector", "dve", lambda e: e.tensor_scalar_min(p.t["srs"][:, :], p.t["srs"][:, :], 1e10))
        p.step("vector", "dve", lambda e: e.tensor_mul(p.t["kvt"][:, :], p.t["skv"][:, :], p.t["srs"][:, :]))

    p.wait("tensor", "dw")
    for i in range(nt):
        for half in range(2):
            kv_tile(i, half)
            p.wait("tensor", "dve")
            p.step("tensor", "pe",
                   lambda e, i=i, half=half: e.matmul(
                       p.t["pTot"][0:1, 512 * half:512 * (half + 1)],
                       p.t["ones_col"][:, :], p.t["kvt"][:, :],
                       start=(i == 0), stop=(i == nt - 1)), chain=False)
    p.wait("scalar", "pe")
    a_tot = p.step("scalar", "act", lambda e: e.activation(p.t["tot_row"][:, :], p.t["pTot"][:, :], AF.Copy))

    emit_allgather(p, p.t["tot_row"][:, :], p.t["ag_s"][:, :], tot_d, ag_d, "act", a_tot)
    p.wait("tensor", "dcc")
    p.raw("tensor",
          lambda e: e.matmul(p.t["pTot"][0:1, 0:512], p.t["sel_s"][:, :], p.t["ag_s"][:, 0:512],
                             start=True, stop=True))
    p.step("tensor", "pe",
           lambda e: e.matmul(p.t["pTot"][0:1, 512:1024], p.t["sel_s"][:, :], p.t["ag_s"][:, 512:1024],
                              start=True, stop=True), chain=False)
    p.wait("scalar", "pe", p.cnt["pe"])
    p.step("scalar", "act", lambda e: e.activation(p.t["pre_row"][:, :], p.t["pTot"][:, :], AF.Copy))

    for i in range(nt):
        p.wait("sync", "dve")
        p.wait("sync", "st")
        p.dma("sync", "dx", x_t[:, i % 2, :],
              x_in.ap().rearrange("(ii pp) d -> pp ii d", pp=128)[:, i, :])
        for half in range(2):
            kv_tile(i, half)
            p.wait("scalar", "pe")
            a_sc = p.step("scalar", "act", lambda e: e.activation(p.t["ssc"][:, :], p.t["pL"][:, :], AF.Exp))
            p.wait("tensor", "act", a_sc)
            for j in range(8):
                fn = lambda e, i=i, half=half, j=j: e.matmul(
                    p.t["pL"][:, :], p.t["xnT"][:, j, 128 * i:128 * (i + 1)],
                    p.t["wr_s"][:, j, 512 * half:512 * (half + 1)],
                    start=(j == 0), stop=(j == 7))
                if j == 7:
                    p.step("tensor", "pe", fn, chain=False)
                else:
                    p.raw("tensor", fn)
            p.wait("scalar", "pe")
            p.step("scalar", "act", lambda e: e.activation(p.t["r_t"][:, :], p.t["pL"][:, :], AF.Sigmoid))
            p.wait("tensor", "dve")
            p.raw("tensor",
                  lambda e, half=half: e.matmul(p.t["pA"][:, :], p.t["triuT_s"][:, :], p.t["kvt"][:, :],
                                                start=True, stop=False))
            if i > 0:
                p.raw("tensor", lambda e, half=half: e.matmul(
                    p.t["pA"][:, :], p.t["ones_row"][0:1, :],
                    p.t["run_row"][0:1, 512 * half:512 * (half + 1)], start=False, stop=False))
            p.step("tensor", "pe", lambda e, half=half: e.matmul(
                p.t["pA"][:, :], p.t["ones_row"][0:1, :],
                p.t["pre_row"][0:1, 512 * half:512 * (half + 1)], start=False, stop=True), chain=False)
            p.step("tensor", "pe",
                   lambda e, i=i, half=half: e.matmul(
                       p.t["pTot"][0:1, 512 * half:512 * (half + 1)],
                       p.t["ones_col"][:, :], p.t["kvt"][:, :],
                       start=True, stop=True), chain=False)
            p.wait("vector", "pe")
            p.step("vector", "dve", lambda e: e.tensor_mul(p.t["str"][:, :], p.t["pA"][:, :], p.t["ssc"][:, :]))
            p.wait("vector", "act")
            p.step("vector", "dve",
                   lambda e, half=half: e.tensor_mul(p.t["y_t"][:, 512 * half:512 * (half + 1)],
                                                     p.t["str"][:, :], p.t["r_t"][:, :]))
            p.wait("vector", "pe")
            p.step("vector", "dve",
                   lambda e, half=half: e.scalar_tensor_tensor(
                       p.t["run_row"][0:1, 512 * half:512 * (half + 1)],
                       p.t["pTot"][0:1, 512 * half:512 * (half + 1)], 1.0,
                       p.t["run_row"][0:1, 512 * half:512 * (half + 1)],
                       ALU.mult, ALU.add))
        mcol = p.t["m_col"]; ssq = p.t["ssq_col"]; rstd = p.t["rstd_col"]
        p.wait("scalar", "dve")
        p.step("scalar", "act", lambda e: e.activation(p.t["sq_scratch"][:, :], p.t["y_t"][:, :], AF.Square,
                                                       accum_out=ssq[:, :]))
        p.step("scalar", "act", lambda e: e.activation(rstd[:, :], ssq[:, :], AF.Ln,
                                                       bias=p.t["c_eps6"][:, :], scale=p.t["c_inv1024"][:, :]))
        p.step("scalar", "act", lambda e: e.activation(rstd[:, :], rstd[:, :], AF.Exp,
                                                       scale=p.t["c_neghalf"][:, :]))
        p.wait("vector", "act")
        p.step("vector", "dve", lambda e: e.tensor_scalar(p.t["yn_bf"][:, :], p.t["y_t"][:, :],
                                                          rstd[:, :], None, ALU.mult))
        p.wait("tensor", "dve")
        pT = [p.t["pT0"], p.t["pT1"]]
        for h in range(2):
            p.wait("tensor", "act")
            for jj in range(4):
                j = 4 * h + jj
                fn = lambda e, h=h, j=j, jj=jj: e.transpose(
                    pT[h][:, 128 * jj:128 * (jj + 1)],
                    p.t["yn_bf"][:, 128 * j:128 * (j + 1)], p.t["ident_s"][:, :])
                if jj == 3:
                    p.step("tensor", "pe", fn, chain=False)
                else:
                    p.raw("tensor", fn)
            p.wait("scalar", "pe")
            p.step("scalar", "act",
                   lambda e, h=h: e.activation(
                       p.t["ynT_t"][:, 4 * h:4 * h + 4, :],
                       pT[h][:, :].rearrange("p (a b) -> p a b", a=4), AF.Copy))
        p.wait("tensor", "act")
        p.wait("vector", "dx")
        for half in range(2):
            for j in range(8):
                fn = lambda e, half=half, j=j: e.matmul(
                    p.t["pO"][:, :], p.t["ynT_t"][:, j, :],
                    p.t["wo_s"][:, j, 512 * half:512 * (half + 1)],
                    start=(j == 0), stop=(j == 7))
                if j == 7:
                    p.step("tensor", "pe", fn, chain=False)
                else:
                    p.raw("tensor", fn)
            p.wait("vector", "pe")
            p.step("vector", "dve",
                   lambda e, i=i, half=half: e.tensor_add(
                       x_t[:, i % 2, 512 * half:512 * (half + 1)], p.t["pO"][:, :],
                       x_t[:, i % 2, 512 * half:512 * (half + 1)]))
            p.wait("tensor", "dve")
        p.wait("sync", "dve")
        p.dma("sync", "st", x_out.ap().rearrange("(ii pp) d -> pp ii d", pp=128)[:, i, :],
              x_t[:, i % 2, :])
    p.wait("sync", "st")
    return p


def build_moe(tloc=TLOC, p=None):
    p = p or Prog("moe", tloc)
    nt = p.nt
    x_in = p.dram("x_in", [tloc, 1024], F32, kind="ExternalInput")
    p.dram("wg1_in", [1024, 256], F32, kind="ExternalInput")
    p.dram("bg1_in", [128, 2], F32, kind="ExternalInput")
    p.dram("wg2_in", [256, 40], F32, kind="ExternalInput")
    p.dram("bg2_in", [1, 40], F32, kind="ExternalInput")
    p.dram("we_in", [20, 1024, 1024], F32, kind="ExternalInput")
    p.dram("be_in", [20, 1024], F32, kind="ExternalInput")
    p.dram("ident_in", [128, 128], BF16, kind="ExternalInput")
    p.dram("identf_in", [128, 128], F32, kind="ExternalInput")
    x_out = p.dram("x_out", [tloc, 1024], F32, kind="ExternalOutput")

    p.sbuf("acc", [128, nt, 1024])
    p.sbuf("xnT", [128, 8, tloc], BF16)
    p.sbuf("xn_bf", [128, 1024], BF16)
    p.sbuf("sq_scratch", [128, 1024])
    p.sbuf("we_s", [128, 2, 8, 1024], BF16)
    p.sbuf("wg1_s", [128, 8, 256])
    p.sbuf("wg2_s", [128, 2, 40])
    p.sbuf("xn_32", [128, 1024])
    p.sbuf("xnT32_t", [128, 8, 128])
    p.sbuf("hT32_t", [128, 2, 128])
    p.sbuf("bg1_col", [128, 2]); p.sbuf("bg2_row", [1, 40])
    p.sbuf("be_s", [20, 1024])
    p.sbuf("gates", [128, nt, 20])
    p.sbuf("gT_s", [20, tloc])
    p.sbuf("prob_t", [128, 20]); p.sbuf("mask_t", [128, 20]); p.sbuf("tmp_t", [128, 20])
    p.sbuf("m1_col", [128, 1]); p.sbuf("sum_col", [128, 1])
    p.sbuf("ident_s", [128, 128], BF16); p.sbuf("identf_s", [128, 128], F32)
    p.sbuf("ones_col", [128, 1])
    for nm in ["m_col", "ssq_col", "rstd_col", "nb_col"]:
        p.sbuf(nm, [128, 1])
    p.psum("pT0", [128, 512], BF16); p.psum("pT1", [128, 512], BF16)
    p.psum("pB", [128, 512])
    p.psum("pO0", [128, 512]); p.psum("pO1", [128, 512])

    for s in ["dx", "dwh", "dm", "dwe0", "dwe1", "pe", "dve", "act", "pool", "st"]:
        p.sem(s)
    emit_consts(p)
    p.step("vector", "dve", lambda e: e.memset(p.t["ones_col"][:, :], 1.0))

    p.dma("sync", "dx", p.t["acc"][:, :, :],
          x_in.ap().rearrange("(i p) d -> p i d", p=128))
    for nm, src_ in [("bg1_col", "bg1_in"), ("bg2_row", "bg2_in"),
                     ("be_s", "be_in"), ("ident_s", "ident_in"), ("identf_s", "identf_in")]:
        p.dma("sync", "dm", p.t[nm][:, :], p.t[src_].ap())
    p.dma("sync", "dwh", p.t["wg1_s"][:, :, :],
          p.t["wg1_in"].ap().rearrange("(j p) h -> p j h", p=128))
    p.dma("sync", "dwh", p.t["wg2_s"][:, :, :],
          p.t["wg2_in"].ap().rearrange("(j p) h -> p j h", p=128))
    p.dma("gpsimd", "dwe0", p.t["we_s"][:, 0, :, :],
          p.t["we_in"].ap()[0].rearrange("(j p) e -> p j e", p=128))

    p.wait("vector", "dx")
    p.wait("scalar", "dx")
    p.wait("tensor", "dm")
    p.wait("tensor", "dwh")

    def gate_tile(p, i):
        p.wait("tensor", "dve")
        for h4 in range(2):
            p.wait("tensor", "act")
            for jj in range(4):
                j = 4 * h4 + jj
                fn = lambda e, j=j, jj=jj: e.transpose(
                    p.t["pB"][:, 128 * jj:128 * (jj + 1)],
                    p.t["xn_32"][:, 128 * j:128 * (j + 1)], p.t["identf_s"][:, :])
                if jj == 3:
                    p.step("tensor", "pe", fn, chain=False)
                else:
                    p.raw("tensor", fn)
            p.wait("scalar", "pe")
            p.step("scalar", "act",
                   lambda e, h4=h4: e.activation(
                       p.t["xnT32_t"][:, 4 * h4:4 * h4 + 4, :],
                       p.t["pB"][:, :].rearrange("p (a b) -> p a b", a=4), AF.Copy))
        for ht in range(2):
            p.wait("tensor", "act")
            for j in range(8):
                fn = lambda e, ht=ht, j=j: e.matmul(
                    p.t["pB"][:, 0:128], p.t["wg1_s"][:, j, 128 * ht:128 * (ht + 1)],
                    p.t["xnT32_t"][:, j, :], start=(j == 0), stop=(j == 7))
                if j == 7:
                    p.step("tensor", "pe", fn, chain=False)
                else:
                    p.raw("tensor", fn)
            p.wait("scalar", "pe")
            p.step("scalar", "act",
                   lambda e, ht=ht: e.activation(
                       p.t["hT32_t"][:, ht, :], p.t["pB"][:, 0:128], AF.Gelu,
                       bias=p.t["bg1_col"][:, ht:ht + 1]))
        p.wait("tensor", "act")
        for ht in range(2):
            p.raw("tensor", lambda e, ht=ht: e.matmul(
                p.t["pB"][:, 0:40], p.t["hT32_t"][:, ht, :],
                p.t["wg2_s"][:, ht, :], start=(ht == 0), stop=False))
        p.step("tensor", "pe",
               lambda e: e.matmul(p.t["pB"][:, 0:40], p.t["ones_row"][0:1, :],
                                  p.t["bg2_row"][0:1, :], start=False, stop=True), chain=False)
        p.wait("vector", "pe")
        p.step("vector", "dve", lambda e: e.reduce_max(p.t["m1_col"][:, :], p.t["pB"][:, 0:20],
                                                       axis=AX.X, negate=True))
        p.wait("scalar", "dve")
        p.step("scalar", "act", lambda e: e.activation(p.t["prob_t"][:, :], p.t["pB"][:, 0:20], AF.Exp,
                                                       bias=p.t["m1_col"][:, :], accum_out=p.t["sum_col"][:, :]))
        p.wait("vector", "act")
        p.step("vector", "dve", lambda e: e.reciprocal(p.t["sum_col"][:, :], p.t["sum_col"][:, :]))
        p.step("vector", "dve", lambda e: e.tensor_scalar(p.t["prob_t"][:, :], p.t["prob_t"][:, :],
                                                          p.t["sum_col"][:, :], None, ALU.mult))
        p.step("vector", "dve", lambda e: e.reduce_max(p.t["m1_col"][:, :], p.t["prob_t"][:, :], axis=AX.X))
        p.step("vector", "dve", lambda e: e.tensor_scalar(p.t["mask_t"][:, :], p.t["prob_t"][:, :],
                                                          p.t["m1_col"][:, :], None, ALU.is_ge))
        p.step("vector", "dve", lambda e: e.scalar_tensor_tensor(p.t["tmp_t"][:, :], p.t["mask_t"][:, :],
                                                                 -1e30, p.t["prob_t"][:, :], ALU.mult, ALU.add))
        p.step("vector", "dve", lambda e: e.reduce_max(p.t["m1_col"][:, :], p.t["tmp_t"][:, :], axis=AX.X))
        p.step("vector", "dve", lambda e: e.tensor_scalar(p.t["mask_t"][:, :], p.t["prob_t"][:, :],
                                                          p.t["m1_col"][:, :], None, ALU.is_ge))
        p.step("vector", "dve", lambda e, i=i: e.tensor_mul(p.t["gates"][:, i, :], p.t["prob_t"][:, :],
                                                            p.t["mask_t"][:, :]))
        p.wait("tensor", "dve")
        p.step("tensor", "pe", lambda e, i=i: e.transpose(p.t["pB"][0:20, 0:128],
                                                          p.t["gates"][:, i, :], p.t["identf_s"][:, :]),
               chain=False)
        p.wait("scalar", "pe")
        p.step("scalar", "act", lambda e, i=i: e.activation(p.t["gT_s"][:, 128 * i:128 * (i + 1)],
                                                            p.t["pB"][0:20, 0:128], AF.Copy))

    emit_ln_xnt(p, x_name="acc", per_tile_hook=gate_tile, out32=True)

    pO = [p.t["pO0"], p.t["pO1"]]
    for ex in range(20):
        if ex + 1 < 20:
            p.wait("gpsimd", "pe")
            p.dma("gpsimd", "dwe1" if (ex + 1) % 2 else "dwe0",
                  p.t["we_s"][:, (ex + 1) % 2, :, :],
                  p.t["we_in"].ap()[ex + 1].rearrange("(j p) e -> p j e", p=128))
        p.wait("tensor", "dwe1" if ex % 2 else "dwe0")
        for i in range(nt):
            for half in range(2):
                pp = pO[(2 * i + half) % 2]
                p.wait("tensor", "dve")
                for j in range(8):
                    fn = lambda e, ex=ex, i=i, half=half, j=j, pp=pp: e.matmul(
                        pp[:, :], p.t["xnT"][:, j, 128 * i:128 * (i + 1)],
                        p.t["we_s"][:, ex % 2, j, 512 * half:512 * (half + 1)],
                        start=(j == 0), stop=(j == 7))
                    if j == 7:
                        p.step("tensor", "pe", fn, chain=False)
                    else:
                        p.raw("tensor", fn)
                p.wait("vector", "pe")
                p.step("vector", "dve",
                       lambda e, ex=ex, i=i, half=half, pp=pp: e.scalar_tensor_tensor(
                           p.t["acc"][:, i, 512 * half:512 * (half + 1)], pp[:, :],
                           p.t["gates"][:, i, ex:ex + 1],
                           p.t["acc"][:, i, 512 * half:512 * (half + 1)], ALU.mult, ALU.add))
    p.wait("tensor", "act")
    for i in range(nt):
        for half in range(2):
            pp = pO[(2 * i + half) % 2]
            p.wait("tensor", "dve")
            p.step("tensor", "pe",
                   lambda e, i=i, half=half, pp=pp: e.matmul(
                       pp[:, :], p.t["gT_s"][:, 128 * i:128 * (i + 1)],
                       p.t["be_s"][:, 512 * half:512 * (half + 1)],
                       start=True, stop=True), chain=False)
            p.wait("vector", "pe")
            p.step("vector", "dve",
                   lambda e, i=i, half=half, pp=pp: e.scalar_tensor_tensor(
                       p.t["acc"][:, i, 512 * half:512 * (half + 1)], pp[:, :], 1.0,
                       p.t["acc"][:, i, 512 * half:512 * (half + 1)], ALU.mult, ALU.add))
    p.wait("sync", "dve")
    p.dma("sync", "st", x_out.ap().rearrange("(i p) d -> p i d", p=128), p.t["acc"][:, :, :])
    p.wait("sync", "st")
    return p


def moe_in_maps(x_full, inp, prefix="L1_ffn", ncore=NCORE, tloc=TLOC):
    ident = host_ident_bf16()
    identf = np.eye(128, dtype=np.float32)
    maps = []
    for c in range(ncore):
        maps.append({
            "x_in": np.ascontiguousarray(x_full[c * tloc:(c + 1) * tloc]),
            "wg1_in": np.ascontiguousarray(np.asarray(inp[f"{prefix}_wg1"], np.float32)),
            "bg1_in": np.ascontiguousarray(np.asarray(inp[f"{prefix}_bg1"], np.float32).reshape(2, 128).T),
            "wg2_in": np.ascontiguousarray(np.asarray(inp[f"{prefix}_wg2"], np.float32)),
            "bg2_in": np.asarray(inp[f"{prefix}_bg2"], np.float32).reshape(1, 40),
            "we_in": np.ascontiguousarray(np.asarray(inp[f"{prefix}_We"], np.float32)),
            "be_in": np.ascontiguousarray(np.asarray(inp[f"{prefix}_be"], np.float32)),
            "ident_in": ident,
            "identf_in": identf,
        })
    return maps


def build_attnb(tloc=TLOC, groups=None, tfull=None, phases=99, p=None):
    p = p or Prog("attnb", tloc)
    p.groups = groups or GROUPS
    gsize = len(p.groups[0])
    nt = p.nt
    tf = tfull if tfull is not None else tloc * gsize
    nq = tf // 128
    XC = min(512, tloc)
    nxc = tf // XC

    x_in = p.dram("x_in", [tloc, 1024], F32, kind="ExternalInput")
    p.dram("wqT_in", [1024, 256], F32, kind="ExternalInput")
    p.dram("wkT_in", [1024, 256], F32, kind="ExternalInput")
    p.dram("wvT_in", [1024, 256], F32, kind="ExternalInput")
    p.dram("wo2_in", [256, 1024], F32, kind="ExternalInput")
    p.dram("bq_in", [1, 256], F32, kind="ExternalInput")
    p.dram("bk_in", [1, 256], F32, kind="ExternalInput")
    p.dram("bv_in", [1, 256], F32, kind="ExternalInput")
    p.dram("bo_in", [1, 1024], F32, kind="ExternalInput")
    p.dram("gamma_col_in", [128, 8], F32, kind="ExternalInput")
    p.dram("beta_col_in", [128, 8], F32, kind="ExternalInput")
    p.dram("rbT_in", [32, 4, 32], F32, kind="ExternalInput")
    p.dram("m0T_in", [32, tf], F32, kind="ExternalInput")
    p.dram("masks_in", [4, 128, 512], F32, kind="ExternalInput")
    p.dram("flag_in", [1, 512], F32, kind="ExternalInput")
    p.dram("ident_in", [128, 128], BF16, kind="ExternalInput")
    p.dram("identf_in", [128, 128], F32, kind="ExternalInput")
    x_out = p.dram("x_out", [tloc, 1024], F32, kind="ExternalOutput")
    xn_d = p.dram("xn_d", [128, 8 * tloc], BF16)
    xg_d = p.dram("xg_d", [gsize * 128, 8 * tloc], BF16)
    opart_d = p.dram("opart_d", [gsize * 1024, tloc], BF16)
    ors_d = p.dram("ors_d", [1024, tloc], BF16)

    p.sbuf("x_t", [128, 2, 1024])
    p.sbuf("xnt_t", [128, 8, 128], BF16)
    p.sbuf("xn_bf", [128, 1024], BF16)
    p.sbuf("sq_scratch", [128, 1024])
    p.sbuf("xg_s", [128, 2, 8, XC], BF16)
    p.sbuf("wq_s", [128, 8, 256], BF16)
    p.sbuf("wk_s", [128, 8, 256], BF16)
    p.sbuf("wv_s", [128, 8, 256], BF16)
    p.sbuf("wo2_s", [128, 2, 1024], BF16)
    p.sbuf("q_s", [128, tf], BF16)
    p.sbuf("k_s", [128, tf], BF16)
    p.sbuf("v_s", [128, tf // 128, 72], BF16)
    p.sbuf("aoT_s", [128, 2, tf], BF16)
    p.sbuf("m0T_s", [32, tf], BF16)
    p.sbuf("masks_s", [128, 4, 512], BF16)
    p.sbuf("rbT_s", [32, 4, 32], BF16)
    p.sbuf("bq_row", [1, 256]); p.sbuf("bk_row", [1, 256]); p.sbuf("bv_row", [1, 256])
    p.sbuf("bo_row", [1, 1024]); p.sbuf("flag_row", [1, 512])
    p.sbuf("qb_row", [1, 256]); p.sbuf("kb_row", [1, 256]); p.sbuf("vb_row", [1, 256])
    p.sbuf("gamma_col", [128, 8]); p.sbuf("beta_bf", [128, 8], BF16)
    p.sbuf("ident_s", [128, 128], BF16); p.sbuf("identf_s", [128, 128], F32)
    p.sbuf("ones_long", [1, 512])
    p.sbuf("p_bf", [128, 2, 512], BF16)
    p.sbuf("pt_s", [128, 2, 512], BF16)
    p.sbuf("sS", [128, 512])
    p.sbuf("acc", [128, 72])
    p.sbuf("att_bf", [128, 64], BF16)
    p.sbuf("nm_run", [128, 1]); p.sbuf("nm_new", [128, 2]); p.sbuf("f_col", [128, 2])
    p.sbuf("rcol", [128, 1])
    p.sbuf("opart_t", [128, 2, 512], BF16)
    p.sbuf("ors_t", [128, 8, 128], BF16)
    p.sbuf("c_eighth", [128, 1])
    for nm in ["m_col", "ssq_col", "rstd_col", "nb_col"]:
        p.sbuf(nm, [128, 1])

    p.psum("pT0", [128, 512], BF16); p.psum("pT1", [128, 512], BF16)
    p.psum("pS", [128, 512]); p.psum("pS2", [128, 512])
    p.psum("pPT", [128, 1024], BF16)
    p.psum("pAV", [128, 144])
    p.psum("pP0", [128, 512]); p.psum("pP1", [128, 512])

    for s in ["dx", "dw", "dm", "dxn", "dg0", "dg1", "dor", "pe", "dve", "act", "pool",
              "dcc", "cc", "dst", "dst2", "st"]:
        p.sem(s)
    emit_consts(p)
    p.step("vector", "dve", lambda e: e.memset(p.t["c_eighth"][:, :], 0.125))
    p.step("vector", "dve", lambda e: e.memset(p.t["ones_long"][:, :], 1.0))

    def _early_out():
        p.dma("sync", "st", x_out.ap(), x_in.ap())
        p.wait("sync", "st")
        return p

    for i in range(nt):
        pass
    p.dma("sync", "dx", p.t["x_t"][:, 0, :],
          x_in.ap().rearrange("(i p) d -> p i d", p=128)[:, 0, :])
    p.streams["sync"].pop(); p.cnt["dx"] -= 16
    p.dma("gpsimd", "dw", p.t["beta_bf"][:, :], p.t["beta_col_in"].ap())
    for nm, src_ in [("bq_row", "bq_in"), ("bk_row", "bk_in"), ("bv_row", "bv_in"),
                     ("bo_row", "bo_in"), ("flag_row", "flag_in"),
                     ("gamma_col", "gamma_col_in"),
                     ("ident_s", "ident_in"), ("identf_s", "identf_in")]:
        ap = p.t[src_].ap()
        p.dma("sync", "dm", p.t[nm][:, :] if len(p.t[nm].shape) == 2 else p.t[nm][:, :, :], ap)
    p.dma("gpsimd", "dw", p.t["masks_s"][:, :, :],
          p.t["masks_in"].ap().rearrange("m p k -> p m k"))
    p.dma("gpsimd", "dw", p.t["rbT_s"][:, :, :], p.t["rbT_in"].ap())
    p.dma("gpsimd", "dw", p.t["m0T_s"][:, :], p.t["m0T_in"].ap())
    for nm, src_ in [("wq_s", "wqT_in"), ("wk_s", "wkT_in"), ("wv_s", "wvT_in")]:
        p.dma("gpsimd", "dw", p.t[nm][:, :, :],
              p.t[src_].ap().rearrange("(j p) e -> p j e", p=128))
    p.dma("gpsimd", "dw", p.t["wo2_s"][:, :, :],
          p.t["wo2_in"].ap().rearrange("(j p) e -> p j e", p=128))
    p.dma("gpsimd", "dw", p.t["q_s"][64:96, :], p.t["m0T_in"].ap())

    p.wait("tensor", "dw")
    p.wait("tensor", "dm")
    for (wnm, brow, obrow) in [("wq_s", "bq_row", "qb_row"), ("wk_s", "bk_row", "kb_row"),
                               ("wv_s", "bv_row", "vb_row")]:
        for j in range(8):
            fn = lambda e, wnm=wnm, j=j: e.matmul(
                p.t["pS"][0:1, 0:256], p.t["beta_bf"][:, j:j + 1], p.t[wnm][:, j, :],
                start=(j == 0), stop=(j == 7))
            if j == 7:
                p.step("tensor", "pe", fn, chain=False)
            else:
                p.raw("tensor", fn)
        p.wait("vector", "pe")
        p.step("vector", "dve",
               lambda e, brow=brow, obrow=obrow: e.tensor_add(
                   p.t[obrow][:, :], p.t["pS"][0:1, 0:256], p.t[brow][:, :]))
        p.wait("tensor", "dve")
    for wnm in ["wq_s", "wk_s", "wv_s"]:
        for j in range(8):
            p.step("vector", "dve",
                   lambda e, wnm=wnm, j=j: e.tensor_scalar(
                       p.t[wnm][:, j, :], p.t[wnm][:, j, :],
                       p.t["gamma_col"][:, j:j + 1], None, ALU.mult))

    if phases < 2:
        return _early_out()
    x_t = p.t["x_t"]

    def ln_tile_b(i):
        mcol = p.t["m_col"]; ssq = p.t["ssq_col"]; rstd = p.t["rstd_col"]
        nbias = p.t["nb_col"]; sq = p.t["sq_scratch"]; xn_bf = p.t["xn_bf"]
        p.wait("vector", "dx")
        p.wait("vector", "act")
        p.step("vector", "dve", lambda e, i=i: e.reduce_sum(mcol[:, :], x_t[:, i % 2, :], axis=AX.X))
        p.step("vector", "dve", lambda e: e.tensor_scalar(mcol[:, :], mcol[:, :], -1.0 / 1024.0, None, ALU.mult))
        p.wait("scalar", "dve")
        p.step("scalar", "act", lambda e, i=i: e.activation(sq[:, :], x_t[:, i % 2, :], AF.Square,
                                                            bias=mcol[:, :], accum_out=ssq[:, :]))
        p.step("scalar", "act", lambda e: e.activation(rstd[:, :], ssq[:, :], AF.Ln,
                                                       bias=p.t["c_eps5"][:, :], scale=p.t["c_inv1024"][:, :]))
        p.step("scalar", "act", lambda e: e.activation(rstd[:, :], rstd[:, :], AF.Exp,
                                                       scale=p.t["c_neghalf"][:, :]))
        p.wait("vector", "act")
        p.step("vector", "dve", lambda e: e.tensor_mul(nbias[:, :], mcol[:, :], rstd[:, :]))
        p.step("vector", "dve", lambda e, i=i: e.tensor_scalar(xn_bf[:, :], x_t[:, i % 2, :],
                                                               rstd[:, :], nbias[:, :], ALU.mult, ALU.add))
        p.wait("tensor", "dve")
        pT = [p.t["pT0"], p.t["pT1"]]
        for h in range(2):
            p.wait("tensor", "act")
            for jj in range(4):
                j = 4 * h + jj
                fn = lambda e, h=h, j=j, jj=jj: e.transpose(
                    pT[h][:, 128 * jj:128 * (jj + 1)],
                    xn_bf[:, 128 * j:128 * (j + 1)], p.t["ident_s"][:, :])
                if jj == 3:
                    p.step("tensor", "pe", fn, chain=False)
                else:
                    p.raw("tensor", fn)
            p.wait("scalar", "pe")
            p.step("scalar", "act",
                   lambda e, h=h: e.activation(
                       p.t["xnt_t"][:, 4 * h:4 * h + 4, :],
                       pT[h][:, :].rearrange("p (a b) -> p a b", a=4), AF.Copy))
        p.wait("sync", "act")
        p.dma("sync", "dxn", xn_view[:, :, 128 * i:128 * (i + 1)], p.t["xnt_t"][:, :, :])

    xn_view = xn_d.ap().rearrange("p (j t) -> p j t", j=8)
    for i in range(nt):
        p.wait("sync", "dve")
        p.dma("sync", "dx", x_t[:, i % 2, :],
              x_in.ap().rearrange("(ii pp) d -> pp ii d", pp=128)[:, i, :])
        if i > 0:
            p.wait("scalar", "dxn", 16 * i)
        ln_tile_b(i)

    if phases < 3:
        return _early_out()
    p.wait("gpsimd", "dxn")
    p.step("gpsimd", "cc",
           lambda e: e.collective_compute(
               "AllGather", ALU.bypass, replica_groups=p.groups,
               ins=[xn_d.ap().opt()], outs=[xg_d.ap().opt()]))

    xg_view = xg_d.ap().rearrange("(r p) (j t) -> r p j t", p=128, j=8)

    p.step("vector", "dve", lambda e: e.memset(p.t["v_s"][:, :, :], 0.0))
    p.step("vector", "dve", lambda e: e.memset(p.t["v_s"][:, :, 64:65], 1.0))
    p.wait("scalar", "dve")

    if phases < 4:
        return _early_out()
    nheads_run = 1 if phases == 4 or phases == 5 else 4
    it_slot_free = {0: (0, 0), 1: (0, 0)}
    it_ctr = 0
    _nm0 = p.name
    for hl in range(nheads_run):
        p.name = _nm0 + "Zproj"
        for cx in range(nxc):
            slot = cx % 2
            sem = "dg0" if slot == 0 else "dg1"
            p.wait("gpsimd", "pe")
            if hl == 0 and cx == 0:
                p.wait("gpsimd", "cc")
            o0 = (cx * XC) % tloc
            p.dma("gpsimd", sem, p.t["xg_s"][:, slot, :, :],
                  xg_view[(cx * XC) // tloc, :, :, o0:o0 + XC])
            p.wait("tensor", sem)
            for (wnm, brow, dst) in [("wq_s", "qb_row", "q_s"), ("wk_s", "kb_row", "k_s")]:
                p.wait("tensor", "act")
                for j in range(8):
                    fn = lambda e, wnm=wnm, j=j, hl=hl, slot=slot: e.matmul(
                        p.t["pS"][0:64, 0:XC], p.t[wnm][:, j, 64 * hl:64 * (hl + 1)],
                        p.t["xg_s"][:, slot, j, :], start=(j == 0), stop=False)
                    p.raw("tensor", fn)
                p.step("tensor", "pe",
                       lambda e, brow=brow, hl=hl: e.matmul(
                           p.t["pS"][0:64, 0:XC], p.t[brow][0:1, 64 * hl:64 * (hl + 1)],
                           p.t["ones_long"][0:1, 0:XC], start=False, stop=True), chain=False)
                p.wait("scalar", "pe")
                if dst == "q_s":
                    p.step("scalar", "act",
                           lambda e, cx=cx: e.activation(p.t["q_s"][0:64, XC * cx:XC * (cx + 1)],
                                                         p.t["pS"][0:64, 0:XC], AF.Copy,
                                                         scale=p.t["c_eighth"][0:64, :]))
                else:
                    p.step("scalar", "act",
                           lambda e, cx=cx: e.activation(p.t["k_s"][0:64, XC * cx:XC * (cx + 1)],
                                                         p.t["pS"][0:64, 0:XC], AF.Copy))
            p.wait("tensor", "act")
            p.step("tensor", "pe",
                   lambda e, hl=hl, cx=cx: e.matmul(
                       p.t["pS"][64:96, 0:XC], p.t["rbT_s"][:, hl, :],
                       p.t["m0T_s"][:, XC * cx:XC * (cx + 1)], start=True, stop=True),
                   chain=False)
            p.wait("scalar", "pe")
            p.step("scalar", "act",
                   lambda e, cx=cx: e.activation(p.t["k_s"][64:96, XC * cx:XC * (cx + 1)],
                                                 p.t["pS"][64:96, 0:XC], AF.Copy))
            for vv in range(XC // 128):
                p.wait("tensor", "act")
                for j in range(8):
                    fn = lambda e, j=j, hl=hl, slot=slot, vv=vv: e.matmul(
                        p.t["pAV"][:, 0:64], p.t["xg_s"][:, slot, j, 128 * vv:128 * (vv + 1)],
                        p.t["wv_s"][:, j, 64 * hl:64 * (hl + 1)], start=(j == 0), stop=False)
                    p.raw("tensor", fn)
                p.step("tensor", "pe",
                       lambda e, hl=hl: e.matmul(
                           p.t["pAV"][:, 0:64], p.t["ones_row"][0:1, :],
                           p.t["vb_row"][0:1, 64 * hl:64 * (hl + 1)], start=False, stop=True),
                       chain=False)
                p.wait("scalar", "pe")
                p.step("scalar", "act",
                       lambda e, cx=cx, vv=vv: e.activation(
                           p.t["v_s"][:, (XC // 128) * cx + vv, 0:64], p.t["pAV"][:, 0:64], AF.Copy))
        if phases == 4:
            continue
        p.name = _nm0 + "Zflash"
        head_base = (p.cnt["act"], p.cnt["dve"])
        chunks = [(qi, kc) for qi in range(nq) for kc in range(qi // 4 + 1)]
        s_pe = {0: 0, 1: 0}
        av_pe = {0: 0, 1: 0}
        tr_pe = {0: 0, 1: 0}
        acc_dve = {0: 0, 1: 0}

        def emit_S(ci):
            qi, kc = chunks[ci]
            slot = ci % 2
            pSb = "pS" if slot == 0 else "pS2"
            fa, fd = it_slot_free[slot]
            p.wait("tensor", "act", max(fa, head_base[0]))
            p.wait("tensor", "dve", max(fd, head_base[1]))
            p.step("tensor", "pe",
                   lambda e, qi=qi, kc=kc, pSb=pSb: e.matmul(
                       p.t[pSb][:, :], p.t["q_s"][0:96, 128 * qi:128 * (qi + 1)],
                       p.t["k_s"][0:96, 512 * kc:512 * (kc + 1)], start=True, stop=True),
                   chain=False)
            s_pe[slot] = p.cnt["pe"]

        def emit_acc(ci):
            slot = ci % 2
            p.wait("vector", "pe", av_pe[slot])
            p.wait("vector", "act")
            p.step("vector", "dve",
                   lambda e, slot=slot: e.scalar_tensor_tensor(
                       p.t["acc"][:, :], p.t["acc"][:, :],
                       p.t["f_col"][:, slot:slot + 1],
                       p.t["pAV"][:, 72 * slot:72 * slot + 72],
                       ALU.mult, ALU.add))
            acc_dve[slot] = p.cnt["dve"]

        def emit_out(qi):
            p.step("vector", "dve", lambda e: e.reciprocal(p.t["rcol"][:, :], p.t["acc"][:, 64:65]))
            p.step("vector", "dve",
                   lambda e: e.tensor_scalar(p.t["att_bf"][:, :], p.t["acc"][:, 0:64],
                                             p.t["rcol"][:, :], None, ALU.mult))
            p.wait("tensor", "dve")
            p.wait("tensor", "act")
            b0 = 64 * (hl % 2)
            p.step("tensor", "pe",
                   lambda e, b0=b0: e.transpose(p.t["pPT"][b0:b0 + 64, 0:128], p.t["att_bf"][:, :],
                                                p.t["ident_s"][:, :]),
                   chain=False)
            p.wait("scalar", "pe")
            p.step("scalar", "act",
                   lambda e, hl=hl, qi=qi, b0=b0: e.activation(
                       p.t["aoT_s"][b0:b0 + 64, hl // 2, 128 * qi:128 * (qi + 1)],
                       p.t["pPT"][b0:b0 + 64, 0:128], AF.Copy))

        emit_S(0)
        for ci, (qi, kc) in enumerate(chunks):
            slot = ci % 2
            pSb = "pS" if slot == 0 else "pS2"
            if kc == 0:
                if ci > 0:
                    emit_acc(ci - 1)
                    emit_out(qi - 1)
                p.wait("vector", "act")
                p.step("vector", "dve", lambda e: e.memset(p.t["acc"][:, :], 0.0))
                p.step("vector", "dve", lambda e, pv=1 - ci % 2: e.memset(
                    p.t["nm_new"][:, pv:pv + 1], 1e30))
            diag = (kc == qi // 4)
            p.wait("vector", "pe", s_pe[slot])
            if diag:
                p.step("vector", "dve",
                       lambda e, qi=qi, pSb=pSb: e.tensor_add(p.t["sS"][:, :], p.t[pSb][:, :],
                                                              p.t["masks_s"][:, qi % 4, :]))
                src_s = "sS"
            else:
                src_s = pSb
            p.step("vector", "dve",
                   lambda e, src_s=src_s, slot=slot: e.reduce_max(
                       p.t["nm_new"][:, slot:slot + 1], p.t[src_s][:, :], axis=AX.X, negate=True))
            p.step("vector", "dve",
                   lambda e, slot=slot: e.tensor_tensor(p.t["nm_new"][:, slot:slot + 1],
                                                        p.t["nm_new"][:, slot:slot + 1],
                                                        p.t["nm_new"][:, 1 - slot:2 - slot], ALU.min))
            p.step("vector", "dve",
                   lambda e, slot=slot: e.tensor_sub(p.t["f_col"][:, slot:slot + 1],
                                                     p.t["nm_new"][:, slot:slot + 1],
                                                     p.t["nm_new"][:, 1 - slot:2 - slot]))
            if kc > 0:
                emit_acc(ci - 1)
            p.wait("scalar", "dve")
            p.step("scalar", "act",
                   lambda e, slot=slot: e.activation(p.t["f_col"][:, slot:slot + 1],
                                                     p.t["f_col"][:, slot:slot + 1], AF.Exp))
            p.wait("scalar", "pe", max(tr_pe[slot], s_pe[slot]))
            p.step("scalar", "act",
                   lambda e, src_s=src_s, slot=slot: e.activation(
                       p.t["p_bf"][:, slot, :], p.t[src_s][:, :], AF.Exp,
                       bias=p.t["nm_new"][:, slot:slot + 1]))
            it_slot_free[slot] = (p.cnt["act"], p.cnt["dve"])
            if ci + 1 < len(chunks):
                emit_S(ci + 1)
            p.wait("tensor", "act")
            for jj in range(4):
                fn = lambda e, jj=jj, slot=slot: e.transpose(
                    p.t["pPT"][:, 512 * slot + 128 * jj:512 * slot + 128 * (jj + 1)],
                    p.t["p_bf"][:, slot, 128 * jj:128 * (jj + 1)], p.t["ident_s"][:, :])
                if jj == 3:
                    p.step("tensor", "pe", fn, chain=False)
                else:
                    p.raw("tensor", fn)
            tr_pe[slot] = p.cnt["pe"]
            p.wait("scalar", "pe")
            p.step("scalar", "act",
                   lambda e, slot=slot: e.activation(
                       p.t["pt_s"][:, slot, :],
                       p.t["pPT"][:, 512 * slot:512 * (slot + 1)], AF.Copy))
            p.wait("tensor", "act")
            p.wait("tensor", "dve", acc_dve[slot])
            for kk in range(4):
                fn = lambda e, kc=kc, kk=kk, slot=slot: e.matmul(
                    p.t["pAV"][:, 72 * slot:72 * slot + 72],
                    p.t["pt_s"][:, slot, 128 * kk:128 * (kk + 1)],
                    p.t["v_s"][:, 4 * kc + kk, :], start=(kk == 0), stop=(kk == 3))
                if kk == 3:
                    p.step("tensor", "pe", fn, chain=False)
                else:
                    p.raw("tensor", fn)
            av_pe[slot] = p.cnt["pe"]
        emit_acc(len(chunks) - 1)
        emit_out(nq - 1)
    p.name = _nm0 + "Zout"

    if phases < 7:
        return _early_out()
    pP = [p.t["pP0"], p.t["pP1"]]
    for cx in range(nxc):
        for et in range(8):
            oslot = (cx * 8 + et) % 2
            osem = "dst" if oslot == 0 else "dst2"
            p.wait("tensor", "act")
            p.wait("tensor", "dve")
            for gi, hl in enumerate([0, 2]):
                p.raw("tensor", lambda e, hl=hl, et=et, cx=cx, gi=gi: e.matmul(
                    pP[0][:, 0:XC], p.t["wo2_s"][0:64, hl // 2, 128 * et:128 * (et + 1)],
                    p.t["aoT_s"][0:64, hl // 2, XC * cx:XC * (cx + 1)],
                    start=(gi == 0), stop=False))
            p.step("tensor", "pe",
                   lambda e, et=et: e.matmul(
                       pP[0][:, 0:XC], p.t["bo_row"][0:1, 128 * et:128 * (et + 1)],
                       p.t["flag_row"][0:1, 0:XC], start=False, stop=True), chain=False)
            for gi, hl in enumerate([1, 3]):
                fn = lambda e, hl=hl, et=et, cx=cx, gi=gi: e.matmul(
                    pP[1][:, 0:XC], p.t["wo2_s"][64:128, hl // 2, 128 * et:128 * (et + 1)],
                    p.t["aoT_s"][64:128, hl // 2, XC * cx:XC * (cx + 1)],
                    start=(gi == 0), stop=(gi == 1))
                if gi == 1:
                    p.step("tensor", "pe", fn, chain=False)
                else:
                    p.raw("tensor", fn)
            p.wait("scalar", "pe")
            p.wait("scalar", osem)
            p.step("scalar", "act",
                   lambda e, oslot=oslot: e.activation(
                       p.t["opart_t"][:, oslot, 0:XC], pP[0][:, 0:XC], AF.Copy))
            p.wait("vector", "act")
            p.step("vector", "dve",
                   lambda e, oslot=oslot: e.scalar_tensor_tensor(
                       p.t["opart_t"][:, oslot, 0:XC], pP[1][:, 0:XC], 1.0,
                       p.t["opart_t"][:, oslot, 0:XC], ALU.mult, ALU.add))
            p.wait("sync", "dve")
            o0 = (cx * XC) % tloc
            p.dma("sync", osem,
                  opart_d.ap().rearrange("(r j pp) t -> r j pp t", j=8, pp=128)
                  [(cx * XC) // tloc, et, :, o0:o0 + XC],
                  p.t["opart_t"][:, oslot, 0:XC])
    if phases < 8:
        return _early_out()
    p.wait("gpsimd", "dst")
    p.wait("gpsimd", "dst2")
    p.step("gpsimd", "cc",
           lambda e: e.collective_compute(
               "ReduceScatter", ALU.add, replica_groups=p.groups,
               ins=[opart_d.ap().opt()], outs=[ors_d.ap().opt()]))

    if phases < 9:
        return _early_out()
    ors_view = ors_d.ap().rearrange("(j pp) t -> pp j t", pp=128)
    for i in range(nt):
        p.wait("sync", "dve")
        p.wait("sync", "st")
        if i < 2:
            p.wait("sync", "cc")
        p.dma("sync", "dx", x_t[:, i % 2, :],
              x_in.ap().rearrange("(ii pp) d -> pp ii d", pp=128)[:, i, :])
        p.dma("sync", "dor", p.t["ors_t"][:, :, :], ors_view[:, :, 128 * i:128 * (i + 1)])
        p.wait("tensor", "dor")
        p.wait("tensor", "dve")
        p.wait("tensor", "act")
        for h in range(2):
            p.wait("tensor", "dve")
            for jj in range(4):
                j = 4 * h + jj
                fn = lambda e, h=h, j=j, jj=jj: e.transpose(
                    p.t["pPT"][:, 512 * h + 128 * jj:512 * h + 128 * (jj + 1)],
                    p.t["ors_t"][:, j, :], p.t["ident_s"][:, :])
                if jj == 3:
                    p.step("tensor", "pe", fn, chain=False)
                else:
                    p.raw("tensor", fn)
            p.wait("vector", "pe")
            p.wait("vector", "dx")
            p.step("vector", "dve",
                   lambda e, i=i, h=h: e.tensor_add(
                       x_t[:, i % 2, 512 * h:512 * (h + 1)],
                       p.t["pPT"][:, 512 * h:512 * (h + 1)],
                       x_t[:, i % 2, 512 * h:512 * (h + 1)]))
        p.wait("sync", "dve")
        p.dma("sync", "st", x_out.ap().rearrange("(ii pp) d -> pp ii d", pp=128)[:, i, :],
              x_t[:, i % 2, :])
    p.wait("sync", "st")
    return p


def host_interp_m0(n=T, s=32):
    pos = np.arange(n, dtype=np.float32) * ((s - 1) / (n - 1))
    i0 = np.floor(pos).astype(np.int64)
    i1 = np.minimum(i0 + 1, s - 1)
    frac = (pos - i0.astype(np.float32)).astype(np.float32)
    M = np.zeros((n, s), np.float32)
    M[np.arange(n), i0] += (1.0 - frac)
    M[np.arange(n), i1] += frac
    M[n - 1, :] = 0.0
    return M


def host_causal_masks():
    masks = np.zeros((4, 128, 512), np.float32)
    for m in range(4):
        q = 128 * m + np.arange(128)[:, None]
        k = np.arange(512)[None, :]
        masks[m] = np.where(k > q, -1e30, 0.0).astype(np.float32)
    return masks


def attnb_in_maps(x_full, inp, prefix="L0_attn_b", ncore=NCORE, tloc=TLOC,
                  cores_per_batch=4, tfull=None, nheads_total=16):
    tf = tfull if tfull is not None else tloc * cores_per_batch
    ident = host_ident_bf16()
    identf = np.eye(128, dtype=np.float32)
    m0T = np.ascontiguousarray(host_interp_m0(tf).T)
    masks = host_causal_masks()
    Wq = np.asarray(inp[f"{prefix}_Wq"], np.float32); Wk = np.asarray(inp[f"{prefix}_Wk"], np.float32)
    Wv = np.asarray(inp[f"{prefix}_Wv"], np.float32); Wo = np.asarray(inp[f"{prefix}_Wo"], np.float32)
    rb = np.asarray(inp[f"{prefix}_rel_bias"], np.float32)
    gamma = np.asarray(inp[f"{prefix}_gamma"], np.float32)
    beta = np.asarray(inp[f"{prefix}_beta"], np.float32)
    nheads_per_core = nheads_total // cores_per_batch
    ecols = 64 * nheads_per_core
    maps = []
    for c in range(ncore):
        crank = c % cores_per_batch
        cols = slice(ecols * crank, ecols * (crank + 1))
        rbh = rb[nheads_per_core * crank:nheads_per_core * (crank + 1)]
        if rbh.shape[0] < 4:
            rbh = np.concatenate([rbh, np.zeros((4 - rbh.shape[0], 32, 32), np.float32)], 0)
        rbT = np.ascontiguousarray(np.transpose(rbh, (2, 0, 1)))
        maps.append({
            "x_in": np.ascontiguousarray(x_full[c * tloc:(c + 1) * tloc]),
            "wqT_in": np.ascontiguousarray(Wq.T[:, cols]),
            "wkT_in": np.ascontiguousarray(Wk.T[:, cols]),
            "wvT_in": np.ascontiguousarray(Wv.T[:, cols]),
            "wo2_in": np.ascontiguousarray(Wo.T[cols, :]),
            "bq_in": np.ascontiguousarray(inp[f"{prefix}_bq"][None, cols]).astype(np.float32),
            "bk_in": np.ascontiguousarray(inp[f"{prefix}_bk"][None, cols]).astype(np.float32),
            "bv_in": np.ascontiguousarray(inp[f"{prefix}_bv"][None, cols]).astype(np.float32),
            "bo_in": np.asarray(inp[f"{prefix}_bo"], np.float32).reshape(1, 1024),
            "gamma_col_in": np.ascontiguousarray(gamma.reshape(8, 128).T),
            "beta_col_in": np.ascontiguousarray(beta.reshape(8, 128).T),
            "rbT_in": rbT,
            "m0T_in": m0T,
            "masks_in": masks,
            "flag_in": np.full((1, 512), 1.0 if crank == 0 else 0.0, np.float32),
            "ident_in": ident,
            "identf_in": identf,
        })
    return maps


FPAD = 640


def build_ffta_a(tloc=TLOC, p=None):
    p = p or Prog("ffta_a", tloc)
    nt = p.nt
    x_in = p.dram("x_in", [tloc, 1024], F32, kind="ExternalInput")
    p.dram("wk_in", [1024, 1024], F32, kind="ExternalInput")
    p.dram("wq_in", [1024, 1024], F32, kind="ExternalInput")
    p.dram("wv_in", [1024, 1024], F32, kind="ExternalInput")
    p.dram("f_in", [1024, 2 * FPAD], F32, kind="ExternalInput")
    p.dram("ginvT_in", [1024, 2 * FPAD], F32, kind="ExternalInput")
    p.dram("woT_in", [1024, 1024], F32, kind="ExternalInput")
    p.dram("ident_in", [128, 128], BF16, kind="ExternalInput")
    fkre_d = p.dram("fkre_d", [FPAD, tloc], F32, kind="ExternalOutput")
    fkim_d = p.dram("fkim_d", [FPAD, tloc], F32, kind="ExternalOutput")
    fqre_d = p.dram("fqre_d", [FPAD, tloc], F32, kind="ExternalOutput")
    fqim_d = p.dram("fqim_d", [FPAD, tloc], F32, kind="ExternalOutput")
    fv_d = p.dram("fv_d", [2 * FPAD, tloc], F32, kind="ExternalOutput")
    h_d = p.dram("h_d", [2 * FPAD, 1024], BF16, kind="ExternalOutput")

    p.sbuf("x_t", [128, 2, 1024])
    p.sbuf("xnT", [128, 8, tloc], BF16)
    p.sbuf("xn_bf", [128, 1024], BF16)
    p.sbuf("sq_scratch", [128, 1024])
    p.sbuf("wbuf", [128, 8, 2 * FPAD])
    p.sbuf("fbuf", [128, 8, 2 * FPAD])
    p.sbuf("gbuf", [128, 8, 2 * FPAD], BF16)
    p.sbuf("ident_s", [128, 128], BF16)
    p.sbuf("spec_t", [128, 512]); p.sbuf("spec_bf", [128, 512], BF16)
    for nm in ["m_col", "ssq_col", "rstd_col", "nb_col"]:
        p.sbuf(nm, [128, 1])
    p.psum("pT0", [128, 512], BF16); p.psum("pT1", [128, 512], BF16)
    p.psum("pG", [128, 512])
    for s in ["dx", "dwh", "dw2h", "dm", "pe", "dve", "act", "pool", "st"]:
        p.sem(s)
    emit_consts(p)

    p.dma("sync", "dm", p.t["ident_s"][:, :], p.t["ident_in"].ap())
    p.dma("sync", "dw2h", p.t["fbuf"][:, :, :],
          p.t["f_in"].ap().rearrange("(j p) f -> p j f", p=128))

    x_t = p.t["x_t"]

    def ln_tile_f(i):
        mcol = p.t["m_col"]; ssq = p.t["ssq_col"]; rstd = p.t["rstd_col"]
        nbias = p.t["nb_col"]; sq = p.t["sq_scratch"]; xn_bf = p.t["xn_bf"]
        p.wait("vector", "dx")
        p.wait("vector", "act")
        p.step("vector", "dve", lambda e, i=i: e.reduce_sum(mcol[:, :], x_t[:, i % 2, :], axis=AX.X))
        p.step("vector", "dve", lambda e: e.tensor_scalar(mcol[:, :], mcol[:, :], -1.0 / 1024.0, None, ALU.mult))
        p.wait("scalar", "dve")
        p.step("scalar", "act", lambda e, i=i: e.activation(sq[:, :], x_t[:, i % 2, :], AF.Square,
                                                            bias=mcol[:, :], accum_out=ssq[:, :]))
        p.step("scalar", "act", lambda e: e.activation(rstd[:, :], ssq[:, :], AF.Ln,
                                                       bias=p.t["c_eps5"][:, :], scale=p.t["c_inv1024"][:, :]))
        p.step("scalar", "act", lambda e: e.activation(rstd[:, :], rstd[:, :], AF.Exp,
                                                       scale=p.t["c_neghalf"][:, :]))
        p.wait("vector", "act")
        p.step("vector", "dve", lambda e: e.tensor_mul(nbias[:, :], mcol[:, :], rstd[:, :]))
        p.step("vector", "dve", lambda e, i=i: e.tensor_scalar(xn_bf[:, :], x_t[:, i % 2, :],
                                                               rstd[:, :], nbias[:, :], ALU.mult, ALU.add))
        p.wait("tensor", "dve")
        pT = [p.t["pT0"], p.t["pT1"]]
        for h in range(2):
            p.wait("tensor", "act")
            for jj in range(4):
                j = 4 * h + jj
                fn = lambda e, h=h, j=j, jj=jj: e.transpose(
                    pT[h][:, 128 * jj:128 * (jj + 1)],
                    xn_bf[:, 128 * j:128 * (j + 1)], p.t["ident_s"][:, :])
                if jj == 3:
                    p.step("tensor", "pe", fn, chain=False)
                else:
                    p.raw("tensor", fn)
            p.wait("scalar", "pe")
            p.step("scalar", "act",
                   lambda e, i=i, h=h: e.activation(
                       p.t["xnT"][:, 4 * h:4 * h + 4, 128 * i:128 * (i + 1)],
                       pT[h][:, :].rearrange("p (a b) -> p a b", a=4), AF.Copy))

    p.wait("tensor", "dm")
    for i in range(nt):
        p.wait("sync", "dve")
        p.dma("sync", "dx", x_t[:, i % 2, :],
              x_in.ap().rearrange("(ii pp) d -> pp ii d", pp=128)[:, i, :])
        ln_tile_f(i)

    nfc = 2 * FPAD // 512
    fchunks = [(0, 512), (512, 1024), (1024, 1280)]
    p.wait("tensor", "dw2h")
    for (wsrc, dsts) in [("wk_in", (fkre_d, fkim_d)), ("wq_in", (fqre_d, fqim_d)),
                         ("wv_in", (fv_d,))]:
        p.wait("sync", "pe")
        p.dma("sync", "dwh", p.t["wbuf"][:, :, 0:1024],
              p.t[wsrc].ap().rearrange("(j p) d -> p j d", p=128))
        p.wait("tensor", "dwh")
        for dj in range(8):
            for (f0, f1) in fchunks:
                p.wait("tensor", "act")
                for ej in range(8):
                    fn = lambda e, dj=dj, ej=ej, f0=f0, f1=f1: e.matmul(
                        p.t["pG"][:, 0:f1 - f0], p.t["wbuf"][:, ej, 128 * dj:128 * (dj + 1)],
                        p.t["fbuf"][:, ej, f0:f1], start=(ej == 0), stop=(ej == 7))
                    if ej == 7:
                        p.step("tensor", "pe", fn, chain=False)
                    else:
                        p.raw("tensor", fn)
                p.wait("scalar", "pe")
                p.step("scalar", "act",
                       lambda e, dj=dj, f0=f0, f1=f1: e.activation(
                           p.t["gbuf"][:, dj, f0:f1], p.t["pG"][:, 0:f1 - f0], AF.Copy))
        TCW = min(512, tloc)
        p.wait("tensor", "act")
        for fi in range(2 * FPAD // 128):
            for tc in range(tloc // TCW):
                p.wait("tensor", "act")
                for dj in range(8):
                    fn = lambda e, fi=fi, tc=tc, dj=dj: e.matmul(
                        p.t["pG"][:, 0:TCW], p.t["gbuf"][:, dj, 128 * fi:128 * (fi + 1)],
                        p.t["xnT"][:, dj, TCW * tc:TCW * (tc + 1)], start=(dj == 0), stop=(dj == 7))
                    if dj == 7:
                        p.step("tensor", "pe", fn, chain=False)
                    else:
                        p.raw("tensor", fn)
                p.wait("scalar", "pe")
                p.wait("scalar", "st")
                p.step("scalar", "act",
                       lambda e: e.activation(p.t["spec_t"][:, 0:TCW], p.t["pG"][:, 0:TCW], AF.Copy))
                p.wait("sync", "act")
                if len(dsts) == 2:
                    dd = dsts[0] if fi < 5 else dsts[1]
                    r0 = 128 * fi if fi < 5 else 128 * (fi - 5)
                else:
                    dd = dsts[0]
                    r0 = 128 * fi
                p.dma("sync", "st", dd.ap()[r0:r0 + 128, TCW * tc:TCW * (tc + 1)],
                      p.t["spec_t"][:, 0:TCW])
    p.wait("sync", "pe")
    p.dma("sync", "dwh", p.t["wbuf"][:, :, :],
          p.t["ginvT_in"].ap().rearrange("(j p) f -> p j f", p=128))
    p.dma("sync", "dwh", p.t["fbuf"][:, :, 0:1024],
          p.t["woT_in"].ap().rearrange("(j p) e -> p j e", p=128))
    p.wait("tensor", "dwh")
    for fi in range(2 * FPAD // 128):
        for ec in range(2):
            p.wait("tensor", "act")
            for dj in range(8):
                fn = lambda e, fi=fi, ec=ec, dj=dj: e.matmul(
                    p.t["pG"][:, :], p.t["wbuf"][:, dj, 128 * fi:128 * (fi + 1)],
                    p.t["fbuf"][:, dj, 512 * ec:512 * (ec + 1)], start=(dj == 0), stop=(dj == 7))
                if dj == 7:
                    p.step("tensor", "pe", fn, chain=False)
                else:
                    p.raw("tensor", fn)
            p.wait("scalar", "pe")
            p.wait("scalar", "st")
            p.step("scalar", "act",
                   lambda e: e.activation(p.t["spec_bf"][:, :], p.t["pG"][:, :], AF.Copy))
            p.wait("sync", "act")
            p.dma("sync", "st", h_d.ap()[128 * fi:128 * (fi + 1), 512 * ec:512 * (ec + 1)],
                  p.t["spec_bf"][:, :])
    p.wait("sync", "st")
    return p


def build_ffta_b(tloc=TLOC, groups=None, p=None):
    p = p or Prog("ffta_b", tloc)
    p.groups = groups or GROUPS
    gsize = len(p.groups[0])
    nt = p.nt
    HW = min(1024, tloc)
    NH = tloc // HW
    TC2 = min(512, HW)
    x_in = p.dram("x_in", [tloc, 1024], F32, kind="ExternalInput")
    fkre_d = p.dram("fkre_d", [FPAD, tloc], F32, kind="ExternalInput")
    fkim_d = p.dram("fkim_d", [FPAD, tloc], F32, kind="ExternalInput")
    fqre_d = p.dram("fqre_d", [FPAD, tloc], F32, kind="ExternalInput")
    fqim_d = p.dram("fqim_d", [FPAD, tloc], F32, kind="ExternalInput")
    fv_d = p.dram("fv_d", [2 * FPAD, tloc], F32, kind="ExternalInput")
    p.dram("perms_in", [10, FPAD], I32, kind="ExternalInput")
    p.dram("h_d", [2 * FPAD, 1024], BF16, kind="ExternalInput")
    p.dram("sel_in", [gsize, 1], F32, kind="ExternalInput")
    p.dram("ident_in", [128, 128], BF16, kind="ExternalInput")
    x_out = p.dram("x_out", [tloc, 1024], F32, kind="ExternalOutput")
    kv_d = p.dram("kv_d", [10, 2 * FPAD, tloc], F32)
    tot_d = p.dram("tot_d", [1, FPAD * 20], F32)
    ag_d = p.dram("ag_d", [gsize, FPAD * 20], F32)
    pre_d = p.dram("pre_d", [1, FPAD * 20], F32)

    p.sbuf("x_t", [128, 2, 1024])
    p.sbuf("h_s", [128, 2 * FPAD // 128, 1024], BF16)
    p.sbuf("fv_ft", [128, 2, HW])
    p.sbuf("idx_s", [128, 5, 10], I32)
    p.sbuf("fkp", [128, 2, HW])
    p.sbuf("fqp", [128, 2, HW])
    p.sbuf("kvp", [128, 2, HW])
    p.sbuf("kv32", [128, 2, HW])
    p.sbuf("kvb", [128, 2, HW])
    p.sbuf("zt", [128, HW])
    p.sbuf("r32", [128, 5, 2, HW])
    p.sbuf("r_bf", [128, 5, 2, HW], BF16)
    p.sbuf("outT_bf", [128, 8, HW], BF16)
    p.sbuf("state_sb", [128, 5, 10, 2])
    p.sbuf("pre_cols", [128, 5, 10, 2])
    p.sbuf("tmp_m", [128, HW])
    p.sbuf("ones_bf", [128, HW])
    p.sbuf("sel_s", [gsize, 1])
    p.sbuf("agc", [gsize, 512])
    p.sbuf("prc", [1, 512])
    p.sbuf("ident_s", [128, 128], BF16)
    p.psum("pT0", [128, 512], BF16); p.psum("pT1", [128, 512], BF16)
    p.psum("pH", [128, 512])
    p.psum("pRow", [1, 512])
    for s in ["dx", "dwh", "dm", "dfv", "dkv", "dkl", "dgt", "pe", "dve", "act",
              "pool", "dcch", "cc", "dpre", "st"]:
        p.sem(s)
    emit_consts(p)
    p.step("vector", "dve", lambda e: e.memset(p.t["ones_bf"][:, :], 1.0))
    p.step("vector", "dve", lambda e: e.memset(p.t["state_sb"][:, :, :, :], 0.0))

    p.dma("sync", "dm", p.t["sel_s"][:, :], p.t["sel_in"].ap())
    p.dma("sync", "dm", p.t["ident_s"][:, :], p.t["ident_in"].ap())
    for c in range(10):
        for ft in range(5):
            p.dma("sync", "dm", p.t["idx_s"][:, ft, c:c + 1],
                  p.t["perms_in"].ap()[c:c + 1, 128 * ft:128 * (ft + 1)])
    p.dma("sync", "dwh", p.t["h_s"][:, :, :],
          p.t["h_d"].ap().rearrange("(ff pp) e -> pp ff e", pp=128))

    fvv = fv_d.ap().rearrange("(ff pp) t -> pp ff t", pp=128)

    for half in range(NH):
        for ft in range(5):
            p.wait("sync", "dve")
            p.dma("sync", "dfv", p.t["fv_ft"][:, 0, :], fvv[:, ft, half * HW:(half + 1) * HW])
            p.dma("sync", "dfv", p.t["fv_ft"][:, 1, :], fvv[:, 5 + ft, half * HW:(half + 1) * HW])
            for c in range(10):
                p.wait("gpsimd", "dve")
                p.wait("gpsimd", "dm")
                p.step("gpsimd", "dgt",
                       lambda e, c=c, ft=ft, half=half: e.indirect_dma_start(
                           p.t["fkp"][:, 0, :], None, fkre_d.ap(),
                           IndirectOffsetOnAxis(p.t["idx_s"][:, ft, c:c + 1], 0),
                           element_offset=half * HW), by=16)
                p.step("gpsimd", "dgt",
                       lambda e, c=c, ft=ft, half=half: e.indirect_dma_start(
                           p.t["fkp"][:, 1, :], None, fkim_d.ap(),
                           IndirectOffsetOnAxis(p.t["idx_s"][:, ft, c:c + 1], 0),
                           element_offset=half * HW), by=16)
                p.wait("vector", "dgt")
                p.wait("vector", "dfv")
                fvr = p.t["fv_ft"][:, 0, :]
                fvi = p.t["fv_ft"][:, 1, :]
                p.step("vector", "dve", lambda e, fvr=fvr: e.tensor_mul(
                    p.t["kvp"][:, 0, :], p.t["fkp"][:, 0, :], fvr))
                p.step("vector", "dve", lambda e, fvi=fvi: e.scalar_tensor_tensor(
                    p.t["tmp_m"][:, :], p.t["fkp"][:, 1, :], -1.0, fvi, ALU.mult, ALU.mult))
                p.step("vector", "dve", lambda e: e.tensor_add(
                    p.t["kvp"][:, 0, :], p.t["kvp"][:, 0, :], p.t["tmp_m"][:, :]))
                p.step("vector", "dve", lambda e, fvi=fvi: e.tensor_mul(
                    p.t["kvp"][:, 1, :], p.t["fkp"][:, 0, :], fvi))
                p.step("vector", "dve", lambda e, fvr=fvr: e.tensor_mul(
                    p.t["tmp_m"][:, :], p.t["fkp"][:, 1, :], fvr))
                p.step("vector", "dve", lambda e: e.tensor_add(
                    p.t["kvp"][:, 1, :], p.t["kvp"][:, 1, :], p.t["tmp_m"][:, :]))
                p.wait("vector", "dkv")
                for comp in range(2):
                    p.step("vector", "dve",
                           lambda e, c=c, ft=ft, comp=comp: e.tensor_tensor_scan(
                               p.t["kv32"][:, comp, :], p.t["ones_bf"][:, :],
                               p.t["kvp"][:, comp, :],
                               p.t["state_sb"][:, ft, c, comp:comp + 1],
                               ALU.mult, ALU.add))
                    p.step("vector", "dve",
                           lambda e, c=c, ft=ft, comp=comp: e.tensor_copy(
                               p.t["state_sb"][:, ft, c, comp:comp + 1],
                               p.t["kv32"][:, comp, HW - 1:HW]))
                p.wait("gpsimd", "dve")
                for comp in range(2):
                    p.dma("gpsimd", "dkv",
                          kv_d.ap()[c, comp * FPAD:(comp + 1) * FPAD, half * HW:(half + 1) * HW]
                          .rearrange("(ff pp) t -> pp ff t", pp=128)[:, ft, :],
                          p.t["kv32"][:, comp, :])

    p.wait("sync", "dve")
    p.dma("sync", "dcch",
          tot_d.ap().rearrange("a (ft pp c m) -> pp a ft (c m)", ft=5, pp=128, c=10),
          p.t["state_sb"][:, :, :, :])
    p.wait("gpsimd", "dcch")
    p.wait("gpsimd", "dkv")
    p.step("gpsimd", "cc",
           lambda e: e.collective_compute(
               "AllGather", ALU.bypass, replica_groups=p.groups,
               ins=[tot_d.ap().opt()], outs=[ag_d.ap().opt()]))
    p.wait("sync", "cc")
    NCH = FPAD * 20 // 512
    for ch in range(NCH):
        p.wait("sync", "act")
        p.wait("sync", "dpre")
        p.dma("sync", "dpre", p.t["agc"][:, :], ag_d.ap()[:, 512 * ch:512 * (ch + 1)])
        p.wait("tensor", "dpre")
        p.wait("tensor", "act")
        p.step("tensor", "pe",
               lambda e: e.matmul(p.t["pRow"][:, :], p.t["sel_s"][:, :],
                                  p.t["agc"][:, :], start=True, stop=True), chain=False)
        p.wait("scalar", "pe")
        p.step("scalar", "act",
               lambda e: e.activation(p.t["prc"][:, :], p.t["pRow"][:, :], AF.Copy))
        p.wait("sync", "act")
        p.dma("sync", "dpre", pre_d.ap()[0:1, 512 * ch:512 * (ch + 1)], p.t["prc"][:, :])
    p.wait("sync", "dpre")
    p.dma("sync", "dpre",
          p.t["pre_cols"][:, :, :, :],
          pre_d.ap().rearrange("a (ft pp c m) -> pp a ft (c m)", ft=5, pp=128, c=10))

    for half in range(NH):
        p.wait("vector", "dpre")
        p.wait("vector", "act")
        p.step("vector", "dve", lambda e: e.memset(p.t["r32"][:, :, :, :], 0.0))
        for c in range(10):
            for ft in range(5):
                p.wait("sync", "dve")
                for comp in range(2):
                    p.dma("sync", "dkl",
                          p.t["kvb"][:, comp, :],
                          kv_d.ap()[c, comp * FPAD:(comp + 1) * FPAD, half * HW:(half + 1) * HW]
                          .rearrange("(ff pp) t -> pp ff t", pp=128)[:, ft, :])
                p.wait("gpsimd", "dve")
                p.step("gpsimd", "dgt",
                       lambda e, c=c, ft=ft, half=half: e.indirect_dma_start(
                           p.t["fqp"][:, 0, :], None, fqre_d.ap(),
                           IndirectOffsetOnAxis(p.t["idx_s"][:, ft, c:c + 1], 0),
                           element_offset=half * HW), by=16)
                p.step("gpsimd", "dgt",
                       lambda e, c=c, ft=ft, half=half: e.indirect_dma_start(
                           p.t["fqp"][:, 1, :], None, fqim_d.ap(),
                           IndirectOffsetOnAxis(p.t["idx_s"][:, ft, c:c + 1], 0),
                           element_offset=half * HW), by=16)
                p.wait("vector", "dgt")
                p.wait("vector", "dkl")
                Pr = p.t["pre_cols"][:, ft, c, 0:1]
                Pi = p.t["pre_cols"][:, ft, c, 1:2]
                rre = p.t["r32"][:, ft, 0, :]
                rim = p.t["r32"][:, ft, 1, :]
                p.step("vector", "dve", lambda e, Pr=Pr: e.scalar_tensor_tensor(
                    p.t["zt"][:, :], p.t["kvb"][:, 0, :], Pr, p.t["fqp"][:, 0, :],
                    ALU.add, ALU.mult))
                p.step("vector", "dve", lambda e, rre=rre: e.tensor_add(rre, rre, p.t["zt"][:, :]))
                p.step("vector", "dve", lambda e, Pi=Pi: e.scalar_tensor_tensor(
                    p.t["zt"][:, :], p.t["kvb"][:, 1, :], Pi, p.t["fqp"][:, 1, :],
                    ALU.add, ALU.mult))
                p.step("vector", "dve", lambda e, rre=rre: e.tensor_add(rre, rre, p.t["zt"][:, :]))
                p.step("vector", "dve", lambda e, Pi=Pi: e.scalar_tensor_tensor(
                    p.t["zt"][:, :], p.t["kvb"][:, 1, :], Pi, p.t["fqp"][:, 0, :],
                    ALU.add, ALU.mult))
                p.step("vector", "dve", lambda e, rim=rim: e.tensor_add(rim, rim, p.t["zt"][:, :]))
                p.step("vector", "dve", lambda e, Pr=Pr: e.scalar_tensor_tensor(
                    p.t["zt"][:, :], p.t["kvb"][:, 0, :], Pr, p.t["fqp"][:, 1, :],
                    ALU.add, ALU.mult))
                p.step("vector", "dve", lambda e, rim=rim: e.tensor_sub(rim, rim, p.t["zt"][:, :]))
        for ft in range(5):
            for comp in range(2):
                p.step("vector", "dve",
                       lambda e, ft=ft, comp=comp: e.tensor_copy(
                           p.t["r_bf"][:, ft, comp, :], p.t["r32"][:, ft, comp, :]))
        p.wait("tensor", "dve")
        p.wait("tensor", "dwh")
        for et in range(8):
            for tc in range(HW // TC2):
                p.wait("tensor", "act")
                for fi in range(10):
                    ftl = fi if fi < 5 else fi - 5
                    comp = 0 if fi < 5 else 1
                    fn = lambda e, et=et, tc=tc, fi=fi, ftl=ftl, comp=comp: e.matmul(
                        p.t["pH"][:, 0:TC2], p.t["h_s"][:, fi, 128 * et:128 * (et + 1)],
                        p.t["r_bf"][:, ftl, comp, TC2 * tc:TC2 * (tc + 1)],
                        start=(fi == 0), stop=(fi == 9))
                    if fi == 9:
                        p.step("tensor", "pe", fn, chain=False)
                    else:
                        p.raw("tensor", fn)
                p.wait("scalar", "pe")
                p.step("scalar", "act",
                       lambda e, et=et, tc=tc: e.activation(
                           p.t["outT_bf"][:, et, TC2 * tc:TC2 * (tc + 1)], p.t["pH"][:, 0:TC2], AF.Copy))
        for it in range(HW // 128):
            i = half * (HW // 128) + it
            p.wait("sync", "dve")
            p.wait("sync", "st")
            p.dma("sync", "dx", p.t["x_t"][:, i % 2, :],
                  x_in.ap().rearrange("(ii pp) d -> pp ii d", pp=128)[:, i, :])
            p.wait("tensor", "act")
            p.wait("tensor", "dve")
            pT = [p.t["pT0"], p.t["pT1"]]
            for h in range(2):
                p.wait("tensor", "dve")
                for jj in range(4):
                    j = 4 * h + jj
                    fn = lambda e, h=h, j=j, jj=jj, it=it: e.transpose(
                        pT[h][:, 128 * jj:128 * (jj + 1)],
                        p.t["outT_bf"][:, j, 128 * it:128 * (it + 1)], p.t["ident_s"][:, :])
                    if jj == 3:
                        p.step("tensor", "pe", fn, chain=False)
                    else:
                        p.raw("tensor", fn)
                p.wait("vector", "pe")
                p.wait("vector", "dx")
                p.step("vector", "dve",
                       lambda e, i=i, h=h: e.tensor_add(
                           p.t["x_t"][:, i % 2, 512 * h:512 * (h + 1)], pT[h][:, :],
                           p.t["x_t"][:, i % 2, 512 * h:512 * (h + 1)]))
            p.wait("sync", "dve")
            p.dma("sync", "st", x_out.ap().rearrange("(ii pp) d -> pp ii d", pp=128)[:, i, :],
                  p.t["x_t"][:, i % 2, :])
    p.wait("sync", "st")
    return p


def tau_of(h):
    return h // 3, 32 * (h % 3)


def build_attnb1(tloc=TLOC, groups=None, debug=False, p=None):
    p = p or Prog("attnb1", tloc)
    dbg_kind = "ExternalOutput" if debug else "Internal"
    p.groups = groups or GROUPS
    gsize = len(p.groups[0])
    nt = p.nt
    QW = min(256, tloc)
    NQ = tloc // QW
    NS = QW // 16
    NCH = QW // 128
    NSC = NS // NCH

    x_in = p.dram("x_in", [tloc, 1024], F32, kind="ExternalInput")
    p.dram("winT_in", [1024, 1024], F32, kind="ExternalInput")
    p.dram("b_in_in", [1, 1024], F32, kind="ExternalInput")
    p.dram("wqkvaT_in", [1024, 3072], F32, kind="ExternalInput")
    p.dram("bqkva_in", [128, 24], F32, kind="ExternalInput")
    p.dram("wy_in", [768, 1024], F32, kind="ExternalInput")
    p.dram("by_in", [128, 8], F32, kind="ExternalInput")
    p.dram("woutT_in", [1024, 1024], F32, kind="ExternalInput")
    p.dram("bout_in", [128, 8], F32, kind="ExternalInput")
    p.dram("h0_in", [128, 3, 2, 32], F32, kind="ExternalInput")
    p.dram("selcol_in", [128, 4], F32, kind="ExternalInput")
    p.dram("flag0_in", [128, 1], F32, kind="ExternalInput")
    p.dram("mask16_in", [128, QW], F32, kind="ExternalInput")
    p.dram("maskTD_in", [128, 256], BF16, kind="ExternalInput")
    p.dram("negmaskT_in", [128, 128], BF16, kind="ExternalInput")
    p.dram("virtmask_in", [128, 256], BF16, kind="ExternalInput")
    p.dram("virtmaskn_in", [128, 256], BF16, kind="ExternalInput")
    p.dram("ident8v_in", [96, 256], BF16, kind="ExternalInput")
    p.dram("ident_in", [128, 128], BF16, kind="ExternalInput")
    x_out = p.dram("x_out", [tloc, 1024], F32, kind="ExternalOutput")
    mdump_d = p.dram("mdump_d", [NQ, 8, 2, 32, 32 * NS], BF16, kind=dbg_kind)
    qg_d = p.dram("qg_d", [NQ, 3, 3, 128, QW], BF16, kind=dbg_kind)
    yti_d = p.dram("yti_d", [NQ, 3, 2, 128, QW], F32, kind=dbg_kind)
    a16_d = p.dram("a16_d", [NQ, 2, 128, 3, NS], F32, kind=dbg_kind)
    ht_d = p.dram("ht_d", [NQ, 128, 3, 2, 32, NS], BF16, kind=dbg_kind)
    stx_d = p.dram("stx_d", [1, 128 * 3 * 2 * 32], F32)
    sta_d = p.dram("sta_d", [1, 128 * 3 * 2], F32)
    agx_d = p.dram("agx_d", [gsize, 128 * 3 * 2 * 32], F32)
    aga_d = p.dram("aga_d", [gsize, 128 * 3 * 2], F32)

    p.sbuf("winT_s", [128, 8, 1024], BF16)
    p.sbuf("wqkva_s", [128, 8, 1536], BF16)
    p.t["woutT_s"] = p.t["winT_s"]
    p.sbuf("b_in_row", [1, 1024])
    p.sbuf("bqkva_col", [128, 24])
    p.sbuf("by_col", [128, 8]); p.sbuf("bout_col", [128, 8])
    p.sbuf("x_t", [128, 2, 1024])
    p.sbuf("xnt_t", [128, 8, 128], BF16)
    p.sbuf("xn_bf", [128, 1024], BF16)
    p.sbuf("sq_scratch", [128, 1024])
    p.sbuf("u_bf", [128, 1024], BF16)
    p.sbuf("uT", [128, 8, QW], BF16)
    p.sbuf("qkv_bf", [128, 3, 6, QW], BF16)
    p.sbuf("a32", [128, 3, 2, QW])
    p.sbuf("g1", [128, 3, QW]); p.sbuf("g2", [128, 3, QW]); p.sbuf("g3", [128, 3, QW])
    p.sbuf("lrho", [128, 3, QW]); p.sbuf("theta", [128, 3, QW])
    p.sbuf("Lrel", [128, 3, QW])
    p.sbuf("Er", [128, 3, QW], BF16); p.sbuf("En", [128, 3, QW], BF16)
    p.sbuf("CUr", [128, 3, QW], BF16); p.sbuf("CUi", [128, 3, QW], BF16)
    p.sbuf("Urt", [128, 3, QW]); p.sbuf("Uit", [128, 3, QW])
    p.sbuf("qtr", [128, 3, QW], BF16); p.sbuf("qti", [128, 3, QW], BF16)
    p.sbuf("nqti", [128, 3, QW], BF16)
    p.sbuf("kbr", [128, 3, QW], BF16); p.sbuf("kbi", [128, 3, QW], BF16)
    p.sbuf("nkbi", [128, 3, QW], BF16)
    p.sbuf("mask16_s", [128, QW])
    p.sbuf("maskTD_s", [128, 256], BF16); p.sbuf("negmaskT_s", [128, 128], BF16)
    p.sbuf("virtmask_s", [128, 256], BF16); p.sbuf("virtmaskn_s", [128, 256], BF16)
    p.sbuf("ident8v_s", [96, 256], BF16)
    p.sbuf("ident_s", [128, 128], BF16)
    p.sbuf("vsmaj", [128, 4, 2, 384], BF16)
    p.sbuf("ptd", [128, 256], BF16)
    p.sbuf("ptim_n", [128, 128], BF16)
    p.sbuf("pvre", [128, 256], BF16); p.sbuf("pvim", [128, 256], BF16)
    p.sbuf("pvim_n", [128, 256], BF16)
    p.sbuf("mev", [128, 512], BF16)
    p.sbuf("yti", [128, 3, 2, QW])
    p.sbuf("Xt", [128, 3, 2, 32, NS], BF16)
    p.sbuf("Ht", [128, 3, 2, 32, NS], BF16)
    p.t["Htb"] = p.t["Ht"]
    p.sbuf("A16r", [128, 3, NS]); p.sbuf("A16i", [128, 3, NS]); p.sbuf("nA16i", [128, 3, NS])
    p.sbuf("hcar", [128, 3, 2, 32])
    p.sbuf("hinit", [128, 3, 2, 32])
    p.sbuf("hcarb", [128, 3, 2, 32], BF16)
    p.sbuf("tq", [128, 3, 2, 32])
    p.sbuf("asum_l", [128, 3, 1]); p.sbuf("asum_t", [128, 3, 1])
    p.sbuf("acc_l", [128, 3, 1]); p.sbuf("acc_t", [128, 3, 1])
    p.sbuf("Acr", [128, 3, 1]); p.sbuf("Aci", [128, 3, 1])
    p.sbuf("agA", [128, 3, 2, 1]); p.sbuf("agX", [128, 3, 2, 32])
    p.sbuf("ytb", [128, 6, QW], BF16)
    p.t["y2z"] = p.t["uT"]
    p.sbuf("outT_bf", [128, 8, QW], BF16)
    p.sbuf("tmp1", [128, 3, QW]); p.sbuf("tmp2", [128, 3, QW])
    p.sbuf("selcol_s", [128, 4]); p.sbuf("flag0_s", [128, 1])
    p.sbuf("h0_s", [128, 3, 2, 32])
    for nm in ["m_col", "ssq_col", "rstd_col", "nb_col"]:
        p.sbuf(nm, [128, 1])
    p.psum("pT0", [128, 512], BF16); p.psum("pT1", [128, 512], BF16)
    p.psum("pP0", [128, 512]); p.psum("pP1", [128, 512])
    p.psum("pS1", [128, 512]); p.psum("pM", [128, 512])
    p.psum("pY", [128, 256])
    for s in ["dx", "dw", "dw2", "dm", "dmd", "dqg", "dyt", "da16", "dht", "dhtl", "dhl",
              "pe", "dve", "act", "pool", "dcch", "cc", "st"]:
        p.sem(s)
    emit_consts(p)

    for nm, src_ in [("b_in_row", "b_in_in"), ("bqkva_col", "bqkva_in"),
                     ("by_col", "by_in"), ("bout_col", "bout_in"),
                     ("selcol_s", "selcol_in"), ("flag0_s", "flag0_in"),
                     ("mask16_s", "mask16_in"), ("maskTD_s", "maskTD_in"),
                     ("negmaskT_s", "negmaskT_in"), ("virtmask_s", "virtmask_in"),
                     ("virtmaskn_s", "virtmaskn_in"), ("ident8v_s", "ident8v_in"),
                     ("ident_s", "ident_in")]:
        h_ = p.t[nm]
        nd = len(h_.shape)
        ap = h_[:, :] if nd == 2 else (h_[:, :, :] if nd == 3 else h_[:, :, :, :])
        p.dma("sync", "dm", ap, p.t[src_].ap())
    p.dma("sync", "dm", p.t["h0_s"][:, :, :, :], p.t["h0_in"].ap())
    p.dma("gpsimd", "dw", p.t["winT_s"][:, :, :],
          p.t["winT_in"].ap().rearrange("(j p) e -> p j e", p=128))

    p.wait("vector", "dm")
    p.step("vector", "dve", lambda e: e.memset(p.t["acc_l"][:, :, :], 0.0))
    p.step("vector", "dve", lambda e: e.memset(p.t["acc_t"][:, :, :], 0.0))

    x_t = p.t["x_t"]
    INV2PI = float(1.0 / (2 * np.pi))

    def fw(fn):
        for tl in range(3):
            p.step("vector", "dve", lambda e, tl=tl: fn(e, tl))

    def facts(fn):
        p.wait("scalar", "dve")
        for tl in range(3):
            p.step("scalar", "act", lambda e, tl=tl: fn(e, tl))
        p.wait("vector", "act")

    for q in range(NQ):
        t0 = q * QW
        for it in range(QW // 128):
            i = t0 // 128 + it
            p.wait("sync", "dve"); p.wait("sync", "st")
            p.dma("sync", "dx", x_t[:, i % 2, :],
                  x_in.ap().rearrange("(ii pp) d -> pp ii d", pp=128)[:, i, :])
            mcol = p.t["m_col"]; ssq = p.t["ssq_col"]; rstd = p.t["rstd_col"]
            nbias = p.t["nb_col"]
            p.wait("vector", "dx")
            p.wait("vector", "act")
            p.step("vector", "dve", lambda e, i=i: e.reduce_sum(mcol[:, :], x_t[:, i % 2, :], axis=AX.X))
            p.step("vector", "dve", lambda e: e.tensor_scalar(mcol[:, :], mcol[:, :], -1.0 / 1024.0, None, ALU.mult))
            p.wait("scalar", "dve")
            p.step("scalar", "act", lambda e, i=i: e.activation(p.t["sq_scratch"][:, :], x_t[:, i % 2, :],
                                                                AF.Square, bias=mcol[:, :], accum_out=ssq[:, :]))
            p.step("scalar", "act", lambda e: e.activation(rstd[:, :], ssq[:, :], AF.Ln,
                                                           bias=p.t["c_eps5"][:, :], scale=p.t["c_inv1024"][:, :]))
            p.step("scalar", "act", lambda e: e.activation(rstd[:, :], rstd[:, :], AF.Exp,
                                                           scale=p.t["c_neghalf"][:, :]))
            p.wait("vector", "act")
            p.step("vector", "dve", lambda e: e.tensor_mul(nbias[:, :], mcol[:, :], rstd[:, :]))
            p.step("vector", "dve", lambda e, i=i: e.tensor_scalar(p.t["xn_bf"][:, :], x_t[:, i % 2, :],
                                                                   rstd[:, :], nbias[:, :], ALU.mult, ALU.add))
            p.wait("tensor", "dve")
            pT = [p.t["pT0"], p.t["pT1"]]
            for h2 in range(2):
                p.wait("tensor", "act")
                for jj in range(4):
                    j = 4 * h2 + jj
                    fn = lambda e, h2=h2, j=j, jj=jj: e.transpose(
                        pT[h2][:, 128 * jj:128 * (jj + 1)],
                        p.t["xn_bf"][:, 128 * j:128 * (j + 1)], p.t["ident_s"][:, :])
                    if jj == 3:
                        p.step("tensor", "pe", fn, chain=False)
                    else:
                        p.raw("tensor", fn)
                p.wait("scalar", "pe")
                p.step("scalar", "act",
                       lambda e, h2=h2: e.activation(
                           p.t["xnt_t"][:, 4 * h2:4 * h2 + 4, :],
                           pT[h2][:, :].rearrange("p (a b) -> p a b", a=4), AF.Copy))
            p.wait("tensor", "dw")
            p.wait("tensor", "act")
            for half in range(2):
                pp = p.t["pP0"] if half == 0 else p.t["pP1"]
                for j in range(8):
                    p.raw("tensor", lambda e, j=j, half=half, pp=pp: e.matmul(
                        pp[:, :], p.t["xnt_t"][:, j, :],
                        p.t["winT_s"][:, j, 512 * half:512 * (half + 1)],
                        start=(j == 0), stop=False))
                p.step("tensor", "pe",
                       lambda e, half=half, pp=pp: e.matmul(
                           pp[:, :], p.t["ones_row"][0:1, :],
                           p.t["b_in_row"][0:1, 512 * half:512 * (half + 1)],
                           start=False, stop=True), chain=False)
                p.wait("scalar", "pe")
                p.step("scalar", "act",
                       lambda e, half=half, pp=pp: e.activation(
                           p.t["u_bf"][:, 512 * half:512 * (half + 1)], pp[:, :], AF.Silu))
            p.wait("tensor", "act")
            for h2 in range(2):
                p.wait("tensor", "act")
                for jj in range(4):
                    j = 4 * h2 + jj
                    fn = lambda e, h2=h2, j=j, jj=jj: e.transpose(
                        pT[h2][:, 128 * jj:128 * (jj + 1)],
                        p.t["u_bf"][:, 128 * j:128 * (j + 1)], p.t["ident_s"][:, :])
                    if jj == 3:
                        p.step("tensor", "pe", fn, chain=False)
                    else:
                        p.raw("tensor", fn)
                p.wait("scalar", "pe")
                p.step("scalar", "act",
                       lambda e, h2=h2, it=it: e.activation(
                           p.t["uT"][:, 4 * h2:4 * h2 + 4, 128 * it:128 * (it + 1)],
                           pT[h2][:, :].rearrange("p (a b) -> p a b", a=4), AF.Copy))

        p.wait("tensor", "act")
        for ct in range(24):
            m, tl = ct // 3, ct % 3
            if ct % 12 == 0:
                hw = ct // 12
                p.wait("gpsimd", "pe")
                p.dma("gpsimd", "dw2", p.t["wqkva_s"][:, :, :],
                      p.t["wqkvaT_in"].ap().rearrange("(j p) e -> p j e", p=128)
                      [:, :, 1536 * hw:1536 * (hw + 1)])
                p.wait("tensor", "dw2")
            pp = p.t["pP0"] if ct % 2 == 0 else p.t["pP1"]
            p.wait("tensor", "act")
            p.wait("tensor", "dve")
            for j in range(8):
                fn = lambda e, ct=ct, j=j, pp=pp: e.matmul(
                    pp[:, 0:QW], p.t["wqkva_s"][:, j, 128 * (ct % 12):128 * (ct % 12 + 1)],
                    p.t["uT"][:, j, :], start=(j == 0), stop=(j == 7))
                if j == 7:
                    p.step("tensor", "pe", fn, chain=False)
                else:
                    p.raw("tensor", fn)
            p.wait("scalar", "pe")
            p.wait("vector", "pe")
            if m < 6:
                p.step("vector", "dve",
                       lambda e, m=m, tl=tl, ct=ct, pp=pp: e.tensor_scalar(
                           p.t["qkv_bf"][:, tl, m, :], pp[:, 0:QW],
                           p.t["bqkva_col"][:, ct:ct + 1], None, ALU.add))
            else:
                p.step("vector", "dve",
                       lambda e, m=m, tl=tl, ct=ct, pp=pp: e.tensor_scalar(
                           p.t["a32"][:, tl, m - 6, :], pp[:, 0:QW],
                           p.t["bqkva_col"][:, ct:ct + 1], None, ALU.add))

        ar = lambda tl: p.t["a32"][:, tl, 0, :]
        ai = lambda tl: p.t["a32"][:, tl, 1, :]
        g1 = lambda tl: p.t["g1"][:, tl, :]
        g2 = lambda tl: p.t["g2"][:, tl, :]
        g3 = lambda tl: p.t["g3"][:, tl, :]
        p.wait("vector", "act")
        p.wait("vector", "dqg")
        p.wait("vector", "da16")
        p.wait("vector", "dyt")
        p.wait("vector", "pe")
        fw(lambda e, tl: e.tensor_mul(g1(tl), ar(tl), ar(tl)))
        fw(lambda e, tl: e.tensor_mul(g2(tl), ai(tl), ai(tl)))
        fw(lambda e, tl: e.tensor_add(g1(tl), g1(tl), g2(tl)))
        facts(lambda e, tl: e.activation(g2(tl), g1(tl), AF.Ln))
        fw(lambda e, tl: e.tensor_scalar_max(g2(tl), g2(tl), -100.0))
        fw(lambda e, tl: e.tensor_scalar_add(g3(tl), g1(tl), 1.0))
        facts(lambda e, tl: e.activation(p.t["tmp1"][:, tl, :], g3(tl), AF.Ln))
        fw(lambda e, tl: e.tensor_sub(p.t["lrho"][:, tl, :], g2(tl), p.t["tmp1"][:, tl, :]))
        fw(lambda e, tl: e.tensor_scalar_max(p.t["lrho"][:, tl, :], p.t["lrho"][:, tl, :], -60.0))
        facts(lambda e, tl: e.activation(g3(tl), g2(tl), AF.Exp, scale=p.t["c_neghalf"][:, :]))
        fw(lambda e, tl: e.tensor_mul(p.t["Urt"][:, tl, :], ar(tl), g3(tl)))
        fw(lambda e, tl: e.tensor_mul(p.t["Uit"][:, tl, :], ai(tl), g3(tl)))
        fw(lambda e, tl: e.reciprocal(p.t["tmp1"][:, tl, :], p.t["Urt"][:, tl, :]))
        fw(lambda e, tl: e.tensor_mul(p.t["tmp2"][:, tl, :], p.t["Uit"][:, tl, :], p.t["tmp1"][:, tl, :]))
        fw(lambda e, tl: e.reciprocal(p.t["tmp1"][:, tl, :], p.t["Uit"][:, tl, :]))
        fw(lambda e, tl: e.tensor_mul(p.t["tmp1"][:, tl, :], p.t["Urt"][:, tl, :], p.t["tmp1"][:, tl, :]))
        fw(lambda e, tl: e.tensor_scalar(p.t["g2"][:, tl, :], p.t["tmp2"][:, tl, :],
                                         1.0, -1.0, ALU.min, ALU.max))
        fw(lambda e, tl: e.tensor_scalar(p.t["tmp1"][:, tl, :], p.t["tmp1"][:, tl, :],
                                         1.0, -1.0, ALU.min, ALU.max))
        p.wait("scalar", "dve")
        for tl in range(3):
            p.step("scalar", "act", lambda e, tl=tl: e.activation(p.t["g2"][:, tl, :],
                                                                  p.t["g2"][:, tl, :], AF.Arctan))
            p.step("scalar", "act", lambda e, tl=tl: e.activation(p.t["tmp1"][:, tl, :],
                                                                  p.t["tmp1"][:, tl, :], AF.Arctan))
            p.step("scalar", "act", lambda e, tl=tl: e.activation(p.t["theta"][:, tl, :],
                                                                  p.t["tmp2"][:, tl, :], AF.Sign))
        p.wait("vector", "act")
        fw(lambda e, tl: e.scalar_tensor_tensor(p.t["tmp1"][:, tl, :], p.t["theta"][:, tl, :],
                                                float(np.pi / 2), p.t["tmp1"][:, tl, :],
                                                ALU.mult, ALU.subtract))
        p.wait("scalar", "dve")
        for tl in range(3):
            p.step("scalar", "act", lambda e, tl=tl: e.activation(p.t["g3"][:, tl, :],
                                                                  p.t["tmp2"][:, tl, :], AF.Abs))
        p.wait("vector", "act")
        fw(lambda e, tl: e.tensor_scalar(p.t["g3"][:, tl, :], p.t["g3"][:, tl, :], 1.0, None, ALU.is_le))
        fw(lambda e, tl: e.tensor_sub(p.t["g2"][:, tl, :], p.t["g2"][:, tl, :], p.t["tmp1"][:, tl, :]))
        fw(lambda e, tl: e.tensor_mul(p.t["g2"][:, tl, :], p.t["g2"][:, tl, :], p.t["g3"][:, tl, :]))
        fw(lambda e, tl: e.tensor_add(p.t["theta"][:, tl, :], p.t["g2"][:, tl, :], p.t["tmp1"][:, tl, :]))
        p.wait("scalar", "dve")
        for tl in range(3):
            p.step("scalar", "act", lambda e, tl=tl: e.activation(p.t["tmp2"][:, tl, :],
                                                                  p.t["Uit"][:, tl, :], AF.Sign))
        p.wait("vector", "act")
        fw(lambda e, tl: e.tensor_scalar(p.t["tmp1"][:, tl, :], p.t["Urt"][:, tl, :], 0.0, None, ALU.is_lt))
        fw(lambda e, tl: e.tensor_mul(p.t["tmp1"][:, tl, :], p.t["tmp1"][:, tl, :], p.t["tmp2"][:, tl, :]))
        fw(lambda e, tl: e.scalar_tensor_tensor(p.t["theta"][:, tl, :], p.t["tmp1"][:, tl, :],
                                                float(np.pi), p.t["theta"][:, tl, :], ALU.mult, ALU.add))
        p.wait("scalar", "dve")
        for tl in range(3):
            p.step("scalar", "act", lambda e, tl=tl: e.activation(p.t["tmp2"][:, tl, :], p.t["lrho"][:, tl, :],
                                                                  AF.Copy, accum_out=p.t["asum_l"][:, tl, :]))
            p.step("scalar", "act", lambda e, tl=tl: e.activation(p.t["tmp2"][:, tl, :], p.t["theta"][:, tl, :],
                                                                  AF.Copy, accum_out=p.t["asum_t"][:, tl, :]))
        p.wait("vector", "act")
        fw(lambda e, tl: e.tensor_add(p.t["acc_l"][:, tl, :], p.t["acc_l"][:, tl, :], p.t["asum_l"][:, tl, :]))
        fw(lambda e, tl: e.tensor_add(p.t["acc_t"][:, tl, :], p.t["acc_t"][:, tl, :], p.t["asum_t"][:, tl, :]))
        fw(lambda e, tl: e.tensor_tensor_scan(p.t["Lrel"][:, tl, :], p.t["mask16_s"][:, :],
                                              p.t["lrho"][:, tl, :], 0.0, ALU.mult, ALU.add))
        fw(lambda e, tl: e.tensor_tensor_scan(p.t["Urt"][:, tl, :], p.t["mask16_s"][:, :],
                                              p.t["theta"][:, tl, :], 0.0, ALU.mult, ALU.add))
        facts(lambda e, tl: e.activation(p.t["Er"][:, tl, :], p.t["Lrel"][:, tl, :], AF.Exp))
        fw(lambda e, tl: e.tensor_scalar(p.t["En"][:, tl, :], p.t["Lrel"][:, tl, :],
                                         -1.0, 85.0, ALU.mult, ALU.min))
        facts(lambda e, tl: e.activation(p.t["En"][:, tl, :], p.t["En"][:, tl, :], AF.Exp))
        fw(lambda e, tl: e.tensor_scalar(p.t["tmp2"][:, tl, :], p.t["Urt"][:, tl, :], INV2PI, None, ALU.mult))
        fw(lambda e, tl: e.tensor_scalar(p.t["tmp2"][:, tl, :], p.t["tmp2"][:, tl, :],
                                         MAGIC_ROUND, -MAGIC_ROUND, ALU.add, ALU.add))
        fw(lambda e, tl: e.scalar_tensor_tensor(p.t["tmp1"][:, tl, :], p.t["tmp2"][:, tl, :],
                                                float(-2 * np.pi), p.t["Urt"][:, tl, :], ALU.mult, ALU.add))
        facts(lambda e, tl: e.activation(p.t["CUi"][:, tl, :], p.t["tmp1"][:, tl, :], AF.Sin))
        fw(lambda e, tl: e.tensor_scalar_add(p.t["Uit"][:, tl, :], p.t["Urt"][:, tl, :], float(np.pi / 2)))
        fw(lambda e, tl: e.tensor_scalar(p.t["tmp2"][:, tl, :], p.t["Uit"][:, tl, :], INV2PI, None, ALU.mult))
        fw(lambda e, tl: e.tensor_scalar(p.t["tmp2"][:, tl, :], p.t["tmp2"][:, tl, :],
                                         MAGIC_ROUND, -MAGIC_ROUND, ALU.add, ALU.add))
        fw(lambda e, tl: e.scalar_tensor_tensor(p.t["tmp1"][:, tl, :], p.t["tmp2"][:, tl, :],
                                                float(-2 * np.pi), p.t["Uit"][:, tl, :], ALU.mult, ALU.add))
        facts(lambda e, tl: e.activation(p.t["CUr"][:, tl, :], p.t["tmp1"][:, tl, :], AF.Sin))
        QR = lambda tl: p.t["qkv_bf"][:, tl, 0, :]
        QI = lambda tl: p.t["qkv_bf"][:, tl, 1, :]
        KR = lambda tl: p.t["qkv_bf"][:, tl, 2, :]
        KI = lambda tl: p.t["qkv_bf"][:, tl, 3, :]
        fw(lambda e, tl: e.tensor_mul(p.t["tmp1"][:, tl, :], QR(tl), p.t["CUr"][:, tl, :]))
        fw(lambda e, tl: e.tensor_mul(p.t["tmp2"][:, tl, :], QI(tl), p.t["CUi"][:, tl, :]))
        fw(lambda e, tl: e.tensor_sub(p.t["tmp1"][:, tl, :], p.t["tmp1"][:, tl, :], p.t["tmp2"][:, tl, :]))
        fw(lambda e, tl: e.tensor_mul(p.t["qtr"][:, tl, :], p.t["tmp1"][:, tl, :], p.t["Er"][:, tl, :]))
        fw(lambda e, tl: e.tensor_mul(p.t["tmp1"][:, tl, :], QR(tl), p.t["CUi"][:, tl, :]))
        fw(lambda e, tl: e.tensor_mul(p.t["tmp2"][:, tl, :], QI(tl), p.t["CUr"][:, tl, :]))
        fw(lambda e, tl: e.tensor_add(p.t["tmp1"][:, tl, :], p.t["tmp1"][:, tl, :], p.t["tmp2"][:, tl, :]))
        fw(lambda e, tl: e.tensor_mul(p.t["qti"][:, tl, :], p.t["tmp1"][:, tl, :], p.t["Er"][:, tl, :]))
        fw(lambda e, tl: e.tensor_scalar(p.t["nqti"][:, tl, :], p.t["qti"][:, tl, :], -1.0, None, ALU.mult))
        fw(lambda e, tl: e.tensor_mul(p.t["tmp1"][:, tl, :], KR(tl), p.t["CUr"][:, tl, :]))
        fw(lambda e, tl: e.tensor_mul(p.t["tmp2"][:, tl, :], KI(tl), p.t["CUi"][:, tl, :]))
        fw(lambda e, tl: e.tensor_add(p.t["tmp1"][:, tl, :], p.t["tmp1"][:, tl, :], p.t["tmp2"][:, tl, :]))
        fw(lambda e, tl: e.tensor_mul(p.t["kbr"][:, tl, :], p.t["tmp1"][:, tl, :], p.t["En"][:, tl, :]))
        fw(lambda e, tl: e.tensor_mul(p.t["tmp1"][:, tl, :], KI(tl), p.t["CUr"][:, tl, :]))
        fw(lambda e, tl: e.tensor_mul(p.t["tmp2"][:, tl, :], KR(tl), p.t["CUi"][:, tl, :]))
        fw(lambda e, tl: e.tensor_sub(p.t["tmp1"][:, tl, :], p.t["tmp1"][:, tl, :], p.t["tmp2"][:, tl, :]))
        fw(lambda e, tl: e.tensor_mul(p.t["kbi"][:, tl, :], p.t["tmp1"][:, tl, :], p.t["En"][:, tl, :]))
        fw(lambda e, tl: e.tensor_scalar(p.t["nkbi"][:, tl, :], p.t["kbi"][:, tl, :], -1.0, None, ALU.mult))
        for tl in range(3):
            p.step("vector", "dve", lambda e, tl=tl: e.tensor_mul(
                p.t["A16r"][:, tl, :],
                p.t["CUr"][:, tl, :].rearrange("p (s x) -> p s x", x=16)[:, :, 15],
                p.t["Er"][:, tl, :].rearrange("p (s x) -> p s x", x=16)[:, :, 15]))
            p.step("vector", "dve", lambda e, tl=tl: e.tensor_mul(
                p.t["A16i"][:, tl, :],
                p.t["CUi"][:, tl, :].rearrange("p (s x) -> p s x", x=16)[:, :, 15],
                p.t["Er"][:, tl, :].rearrange("p (s x) -> p s x", x=16)[:, :, 15]))
        p.wait("sync", "dve")
        for tl in range(3):
            p.dma("sync", "dqg", qg_d.ap()[q, 0, tl, :, :], p.t["qtr"][:, tl, :])
            p.dma("sync", "dqg", qg_d.ap()[q, 1, tl, :, :], p.t["qti"][:, tl, :])
            p.dma("sync", "dqg", qg_d.ap()[q, 2, tl, :, :], p.t["nqti"][:, tl, :])
        p.dma("sync", "da16", a16_d.ap()[q, 0, :, :, :], p.t["A16r"][:, :, :])
        p.dma("sync", "da16", a16_d.ap()[q, 1, :, :, :], p.t["A16i"][:, :, :])

        p.wait("tensor", "dve")
        for ch in range(NCH):
            for comp in range(2):
                p.wait("tensor", "act")
                for tl in range(3):
                    fn = lambda e, ch=ch, comp=comp, tl=tl: e.transpose(
                        p.t["pT0" if comp == 0 else "pT1"][:, 128 * tl:128 * (tl + 1)],
                        p.t["qkv_bf"][:, tl, 4 + comp, 128 * ch:128 * (ch + 1)],
                        p.t["ident_s"][:, :])
                    if tl == 2:
                        p.step("tensor", "pe", fn, chain=False)
                    else:
                        p.raw("tensor", fn)
                p.wait("scalar", "pe")
                p.step("scalar", "act",
                       lambda e, ch=ch, comp=comp: e.activation(
                           p.t["vsmaj"][:, ch, comp, 0:384],
                           p.t["pT0" if comp == 0 else "pT1"][:, 0:384], AF.Copy))

        p.wait("vector", "act")
        p.step("vector", "dve", lambda e: e.memset(p.t["yti"][:, :, :, :], 0.0))
        for ch in range(NCH):
            for h in range(8):
                tl, b0 = tau_of(h)
                vc = 128 * tl + b0
                p.wait("tensor", "act")
                p.wait("tensor", "dve")
                p.raw("tensor", lambda e, ch=ch, tl=tl, b0=b0: e.matmul(
                    p.t["pS1"][:, 0:128], p.t["kbr"][b0:b0 + 32, tl, 128 * ch:128 * (ch + 1)],
                    p.t["qtr"][b0:b0 + 32, tl, 128 * ch:128 * (ch + 1)], start=True, stop=False))
                p.raw("tensor", lambda e, ch=ch, tl=tl, b0=b0: e.matmul(
                    p.t["pS1"][:, 0:128], p.t["nkbi"][b0:b0 + 32, tl, 128 * ch:128 * (ch + 1)],
                    p.t["qti"][b0:b0 + 32, tl, 128 * ch:128 * (ch + 1)], start=False, stop=True))
                p.raw("tensor", lambda e, ch=ch, tl=tl, b0=b0: e.matmul(
                    p.t["pS1"][:, 128:256], p.t["kbi"][b0:b0 + 32, tl, 128 * ch:128 * (ch + 1)],
                    p.t["qtr"][b0:b0 + 32, tl, 128 * ch:128 * (ch + 1)], start=True, stop=False))
                p.raw("tensor", lambda e, ch=ch, tl=tl, b0=b0: e.matmul(
                    p.t["pS1"][:, 128:256], p.t["kbr"][b0:b0 + 32, tl, 128 * ch:128 * (ch + 1)],
                    p.t["qti"][b0:b0 + 32, tl, 128 * ch:128 * (ch + 1)], start=False, stop=True))
                p.raw("tensor", lambda e, ch=ch, tl=tl, b0=b0: e.matmul(
                    p.t["pS1"][:, 256:512], p.t["kbr"][b0:b0 + 32, tl, 128 * ch:128 * (ch + 1)],
                    p.t["ident8v_s"][b0:b0 + 32, :], start=True, stop=True))
                p.step("tensor", "pe", lambda e, ch=ch, tl=tl, b0=b0: e.matmul(
                    p.t["pM"][:, 0:256], p.t["kbi"][b0:b0 + 32, tl, 128 * ch:128 * (ch + 1)],
                    p.t["ident8v_s"][b0:b0 + 32, :], start=True, stop=True), chain=False)
                p.wait("vector", "pe")
                p.step("vector", "dve", lambda e: e.tensor_mul(p.t["ptd"][:, :], p.t["pS1"][:, 0:256],
                                                               p.t["maskTD_s"][:, :]))
                p.step("vector", "dve", lambda e: e.tensor_mul(p.t["ptim_n"][:, :], p.t["pS1"][:, 128:256],
                                                               p.t["negmaskT_s"][:, :]))
                p.step("vector", "dve", lambda e: e.tensor_mul(p.t["pvre"][:, :], p.t["pS1"][:, 256:512],
                                                               p.t["virtmask_s"][:, :]))
                p.step("vector", "dve", lambda e: e.tensor_mul(p.t["pvim"][:, :], p.t["pM"][:, 0:256],
                                                               p.t["virtmask_s"][:, :]))
                p.step("vector", "dve", lambda e: e.tensor_mul(p.t["pvim_n"][:, :], p.t["pM"][:, 0:256],
                                                               p.t["virtmaskn_s"][:, :]))
                p.wait("tensor", "dve")
                p.raw("tensor", lambda e, ch=ch, vc=vc, b0=b0: e.matmul(
                    p.t["pY"][b0:b0 + 32, 0:128], p.t["vsmaj"][:, ch, 0, vc:vc + 32],
                    p.t["ptd"][:, 0:128], start=True, stop=False))
                p.raw("tensor", lambda e, ch=ch, vc=vc, b0=b0: e.matmul(
                    p.t["pY"][b0:b0 + 32, 0:128], p.t["vsmaj"][:, ch, 1, vc:vc + 32],
                    p.t["ptim_n"][:, :], start=False, stop=True))
                p.raw("tensor", lambda e, ch=ch, vc=vc, b0=b0: e.matmul(
                    p.t["pY"][b0:b0 + 32, 128:256], p.t["vsmaj"][:, ch, 1, vc:vc + 32],
                    p.t["ptd"][:, 0:128], start=True, stop=False))
                p.raw("tensor", lambda e, ch=ch, vc=vc, b0=b0: e.matmul(
                    p.t["pY"][b0:b0 + 32, 128:256], p.t["vsmaj"][:, ch, 0, vc:vc + 32],
                    p.t["ptd"][:, 128:256], start=False, stop=True))
                p.raw("tensor", lambda e, ch=ch, vc=vc: e.matmul(
                    p.t["pM"][0:32, 0:256], p.t["vsmaj"][:, ch, 0, vc:vc + 32],
                    p.t["pvre"][:, :], start=True, stop=False))
                p.raw("tensor", lambda e, ch=ch, vc=vc: e.matmul(
                    p.t["pM"][0:32, 0:256], p.t["vsmaj"][:, ch, 1, vc:vc + 32],
                    p.t["pvim_n"][:, :], start=False, stop=True))
                p.raw("tensor", lambda e, ch=ch, vc=vc: e.matmul(
                    p.t["pM"][0:32, 256:512], p.t["vsmaj"][:, ch, 1, vc:vc + 32],
                    p.t["pvre"][:, :], start=True, stop=False))
                p.step("tensor", "pe", lambda e, ch=ch, vc=vc: e.matmul(
                    p.t["pM"][0:32, 256:512], p.t["vsmaj"][:, ch, 0, vc:vc + 32],
                    p.t["pvim"][:, :], start=False, stop=True), chain=False)
                p.wait("vector", "pe")
                p.step("vector", "dve", lambda e, ch=ch, tl=tl, b0=b0: e.scalar_tensor_tensor(
                    p.t["yti"][b0:b0 + 32, tl, 0, 128 * ch:128 * (ch + 1)], p.t["pY"][b0:b0 + 32, 0:128],
                    1.0, p.t["yti"][b0:b0 + 32, tl, 0, 128 * ch:128 * (ch + 1)], ALU.mult, ALU.add))
                p.step("vector", "dve", lambda e, ch=ch, tl=tl, b0=b0: e.scalar_tensor_tensor(
                    p.t["yti"][b0:b0 + 32, tl, 1, 128 * ch:128 * (ch + 1)], p.t["pY"][b0:b0 + 32, 128:256],
                    1.0, p.t["yti"][b0:b0 + 32, tl, 1, 128 * ch:128 * (ch + 1)], ALU.mult, ALU.add))
                p.wait("vector", "dmd")
                p.step("vector", "dve", lambda e: e.tensor_copy(p.t["mev"][0:32, :], p.t["pM"][0:32, :]))
                p.wait("sync", "dve")
                sg0 = NSC * ch
                mv = mdump_d.ap()[q, h, 0, :, :].rearrange("ee (d s) -> ee d s", d=32)
                mv2 = mdump_d.ap()[q, h, 1, :, :].rearrange("ee (d s) -> ee d s", d=32)
                p.dma("sync", "dmd", mv[:, :, sg0:sg0 + NSC],
                      p.t["mev"][0:32, 0:256].rearrange("ee (d s) -> ee d s", d=32))
                p.dma("sync", "dmd", mv2[:, :, sg0:sg0 + NSC],
                      p.t["mev"][0:32, 256:512].rearrange("ee (d s) -> ee d s", d=32))
        p.wait("sync", "dve")
        for tl in range(3):
            for comp in range(2):
                p.dma("sync", "dyt", yti_d.ap()[q, tl, comp, :, :], p.t["yti"][:, tl, comp, :])

    p.wait("vector", "dyt")
    fw(lambda e, tl: e.tensor_scalar_min(p.t["acc_l"][:, tl, :], p.t["acc_l"][:, tl, :], 0.0))
    facts(lambda e, tl: e.activation(p.t["Acr"][:, tl, :], p.t["acc_l"][:, tl, :], AF.Exp))
    fw(lambda e, tl: e.tensor_scalar(p.t["asum_l"][:, tl, :], p.t["acc_t"][:, tl, :], INV2PI, None, ALU.mult))
    fw(lambda e, tl: e.tensor_scalar(p.t["asum_l"][:, tl, :], p.t["asum_l"][:, tl, :],
                                     MAGIC_ROUND, -MAGIC_ROUND, ALU.add, ALU.add))
    fw(lambda e, tl: e.scalar_tensor_tensor(p.t["asum_t"][:, tl, :], p.t["asum_l"][:, tl, :],
                                            float(-2 * np.pi), p.t["acc_t"][:, tl, :], ALU.mult, ALU.add))
    facts(lambda e, tl: e.activation(p.t["Aci"][:, tl, :], p.t["asum_t"][:, tl, :], AF.Sin))
    fw(lambda e, tl: e.tensor_scalar_add(p.t["asum_t"][:, tl, :], p.t["asum_t"][:, tl, :], float(np.pi / 2)))
    fw(lambda e, tl: e.tensor_scalar(p.t["asum_l"][:, tl, :], p.t["asum_t"][:, tl, :], INV2PI, None, ALU.mult))
    fw(lambda e, tl: e.tensor_scalar(p.t["asum_l"][:, tl, :], p.t["asum_l"][:, tl, :],
                                     MAGIC_ROUND, -MAGIC_ROUND, ALU.add, ALU.add))
    fw(lambda e, tl: e.scalar_tensor_tensor(p.t["asum_t"][:, tl, :], p.t["asum_l"][:, tl, :],
                                            float(-2 * np.pi), p.t["asum_t"][:, tl, :], ALU.mult, ALU.add))
    facts(lambda e, tl: e.activation(p.t["asum_t"][:, tl, :], p.t["asum_t"][:, tl, :], AF.Sin))
    fw(lambda e, tl: e.tensor_mul(p.t["Aci"][:, tl, :], p.t["Aci"][:, tl, :], p.t["Acr"][:, tl, :]))
    fw(lambda e, tl: e.tensor_mul(p.t["Acr"][:, tl, :], p.t["asum_t"][:, tl, :], p.t["Acr"][:, tl, :]))

    def recurrence(dump_ht):
        for q in range(NQ):
            p.wait("sync", "dve")
            p.wait("sync", "dmd")
            p.wait("sync", "da16")
            if dump_ht:
                p.wait("sync", "dht")
            p.dma("sync", "dhl", p.t["A16r"][:, :, :], a16_d.ap()[q, 0, :, :, :])
            p.dma("sync", "dhl", p.t["A16i"][:, :, :], a16_d.ap()[q, 1, :, :, :])
            for h in range(8):
                tl, b0 = tau_of(h)
                for comp in range(2):
                    p.dma("sync", "dhl",
                          p.t["Xt"][b0:b0 + 32, tl, comp, :, :],
                          mdump_d.ap()[q, h, comp, :, :].rearrange("ee (d s) -> d ee s", d=32))
            p.wait("vector", "dhl")
            fw(lambda e, tl: e.tensor_scalar(p.t["nA16i"][:, tl, :], p.t["A16i"][:, tl, :],
                                             -1.0, None, ALU.mult))
            for sg in range(NS):
                prevr = (lambda tl: p.t["hcar"][:, tl, 0, :]) if sg == 0 else \
                    (lambda tl, sg=sg: p.t["Ht"][:, tl, 0, :, sg - 1])
                previ = (lambda tl: p.t["hcar"][:, tl, 1, :]) if sg == 0 else \
                    (lambda tl, sg=sg: p.t["Ht"][:, tl, 1, :, sg - 1])
                for tl in range(3):
                    p.step("vector", "dve", lambda e, tl=tl, sg=sg, prevr=prevr: e.tensor_add(
                        p.t["tq"][:, tl, 0, :], prevr(tl), p.t["Xt"][:, tl, 0, :, sg]))
                    p.step("vector", "dve", lambda e, tl=tl, sg=sg, previ=previ: e.tensor_add(
                        p.t["tq"][:, tl, 1, :], previ(tl), p.t["Xt"][:, tl, 1, :, sg]))
                    p.step("vector", "dve", lambda e, tl=tl, sg=sg: e.tensor_scalar(
                        p.t["Ht"][:, tl, 0, :, sg], p.t["tq"][:, tl, 0, :],
                        p.t["A16r"][:, tl, sg:sg + 1], None, ALU.mult))
                    p.step("vector", "dve", lambda e, tl=tl, sg=sg: e.scalar_tensor_tensor(
                        p.t["Ht"][:, tl, 0, :, sg], p.t["tq"][:, tl, 1, :],
                        p.t["nA16i"][:, tl, sg:sg + 1], p.t["Ht"][:, tl, 0, :, sg], ALU.mult, ALU.add))
                    p.step("vector", "dve", lambda e, tl=tl, sg=sg: e.tensor_scalar(
                        p.t["Ht"][:, tl, 1, :, sg], p.t["tq"][:, tl, 1, :],
                        p.t["A16r"][:, tl, sg:sg + 1], None, ALU.mult))
                    p.step("vector", "dve", lambda e, tl=tl, sg=sg: e.scalar_tensor_tensor(
                        p.t["Ht"][:, tl, 1, :, sg], p.t["tq"][:, tl, 0, :],
                        p.t["A16i"][:, tl, sg:sg + 1], p.t["Ht"][:, tl, 1, :, sg], ALU.mult, ALU.add))
            for tl in range(3):
                p.step("vector", "dve", lambda e, tl=tl: e.tensor_copy(
                    p.t["hcar"][:, tl, 0, :], p.t["Ht"][:, tl, 0, :, NS - 1]))
                p.step("vector", "dve", lambda e, tl=tl: e.tensor_copy(
                    p.t["hcar"][:, tl, 1, :], p.t["Ht"][:, tl, 1, :, NS - 1]))
            if dump_ht:
                p.wait("sync", "dve")
                p.dma("sync", "dht", ht_d.ap()[q, :, :, :, :, :], p.t["Ht"][:, :, :, :, :])

    p.wait("vector", "dyt")
    p.step("vector", "dve", lambda e: e.memset(p.t["hcar"][:, :, :, :], 0.0))
    p.step("vector", "dve", lambda e: e.memset(p.t["Xt"][:, :, :, :, :], 0.0))
    p.step("vector", "dve", lambda e: e.memset(p.t["Ht"][:, :, :, :, :], 0.0))
    recurrence(dump_ht=False)
    p.wait("sync", "dve")
    p.dma("sync", "dcch",
          stx_d.ap().rearrange("o (p tl c w) -> p o tl (c w)", p=128, tl=3, c=2),
          p.t["hcar"][:, :, :, :])
    for tl in range(3):
        p.step("vector", "dve", lambda e, tl=tl: e.tensor_copy(
            p.t["agA"][:, tl, 0, :], p.t["Acr"][:, tl, :]))
        p.step("vector", "dve", lambda e, tl=tl: e.tensor_copy(
            p.t["agA"][:, tl, 1, :], p.t["Aci"][:, tl, :]))
    p.wait("sync", "dve")
    p.dma("sync", "dcch",
          sta_d.ap().rearrange("o (p tl c) -> p o (tl c)", p=128, tl=3),
          p.t["agA"][:, :, :, 0])
    p.wait("gpsimd", "dcch")
    p.step("gpsimd", "cc",
           lambda e: e.collective_compute(
               "AllGather", ALU.bypass, replica_groups=p.groups,
               ins=[stx_d.ap().opt()], outs=[agx_d.ap().opt()]))
    p.step("gpsimd", "cc",
           lambda e: e.collective_compute(
               "AllGather", ALU.bypass, replica_groups=p.groups,
               ins=[sta_d.ap().opt()], outs=[aga_d.ap().opt()]))
    p.wait("vector", "cc")
    for tl in range(3):
        for comp in range(2):
            p.step("vector", "dve", lambda e, tl=tl, comp=comp: e.tensor_scalar(
                p.t["hcar"][:, tl, comp, :], p.t["h0_s"][:, tl, comp, :],
                p.t["flag0_s"][:, :], None, ALU.mult))
    agxv = agx_d.ap().rearrange("r (p tl c w) -> r p tl c w", p=128, tl=3, c=2)
    agav = aga_d.ap().rearrange("r (p tl c) -> r p tl c", p=128, tl=3)
    p.wait("sync", "cc")
    for r in range(gsize - 1):
        p.wait("sync", "dve")
        p.dma("sync", "dhl", p.t["agX"][:, :, :, :], agxv[r, :, :, :, :])
        p.dma("sync", "dhl", p.t["agA"][:, :, :, 0], agav[r, :, :, :])
        p.wait("vector", "dhl")
        for tl in range(3):
            for comp in range(2):
                if comp == 0:
                    p.step("vector", "dve", lambda e, tl=tl: e.tensor_scalar_add(
                        p.t["agA"][:, tl, 0, :], p.t["agA"][:, tl, 0, :], -1.0))
                p.step("vector", "dve", lambda e, tl=tl, comp=comp, r=r: e.tensor_scalar(
                    p.t["agA"][:, tl, comp, :], p.t["agA"][:, tl, comp, :],
                    p.t["selcol_s"][:, r:r + 1], None, ALU.mult))
                if comp == 0:
                    p.step("vector", "dve", lambda e, tl=tl: e.tensor_scalar_add(
                        p.t["agA"][:, tl, 0, :], p.t["agA"][:, tl, 0, :], 1.0))
                p.step("vector", "dve", lambda e, tl=tl, comp=comp, r=r: e.tensor_scalar(
                    p.t["agX"][:, tl, comp, :], p.t["agX"][:, tl, comp, :],
                    p.t["selcol_s"][:, r:r + 1], None, ALU.mult))
        for tl in range(3):
            p.step("vector", "dve", lambda e, tl=tl: e.tensor_scalar(
                p.t["tq"][:, tl, 0, :], p.t["hcar"][:, tl, 0, :],
                p.t["agA"][:, tl, 0, 0:1], None, ALU.mult))
            p.step("vector", "dve", lambda e, tl=tl: e.tensor_scalar(
                p.t["tq"][:, tl, 1, :], p.t["hcar"][:, tl, 1, :],
                p.t["agA"][:, tl, 1, 0:1], None, ALU.mult))
            p.step("vector", "dve", lambda e, tl=tl: e.tensor_sub(
                p.t["tq"][:, tl, 0, :], p.t["tq"][:, tl, 0, :], p.t["tq"][:, tl, 1, :]))
            p.step("vector", "dve", lambda e, tl=tl: e.tensor_add(
                p.t["tq"][:, tl, 0, :], p.t["tq"][:, tl, 0, :], p.t["agX"][:, tl, 0, :]))
            p.step("vector", "dve", lambda e, tl=tl: e.tensor_scalar(
                p.t["tq"][:, tl, 1, :], p.t["hcar"][:, tl, 1, :],
                p.t["agA"][:, tl, 0, 0:1], None, ALU.mult))
            p.step("vector", "dve", lambda e, tl=tl: e.scalar_tensor_tensor(
                p.t["tq"][:, tl, 1, :], p.t["hcar"][:, tl, 0, :],
                p.t["agA"][:, tl, 1, 0:1], p.t["tq"][:, tl, 1, :], ALU.mult, ALU.add))
            p.step("vector", "dve", lambda e, tl=tl: e.tensor_add(
                p.t["tq"][:, tl, 1, :], p.t["tq"][:, tl, 1, :], p.t["agX"][:, tl, 1, :]))
            p.step("vector", "dve", lambda e, tl=tl: e.tensor_copy(
                p.t["hcar"][:, tl, 0, :], p.t["tq"][:, tl, 0, :]))
            p.step("vector", "dve", lambda e, tl=tl: e.tensor_copy(
                p.t["hcar"][:, tl, 1, :], p.t["tq"][:, tl, 1, :]))
    for tl in range(3):
        for comp in range(2):
            p.step("vector", "dve", lambda e, tl=tl, comp=comp: e.tensor_copy(
                p.t["hinit"][:, tl, comp, :], p.t["hcar"][:, tl, comp, :]))
    recurrence(dump_ht=True)

    p.wait("gpsimd", "pe")
    p.dma("gpsimd", "dw2", p.t["winT_s"][:, :, :],
          p.t["woutT_in"].ap().rearrange("(j p) e -> p j e", p=128))
    p.dma("gpsimd", "dw2", p.t["wqkva_s"][:, 0:6, 0:1024],
          p.t["wy_in"].ap().rearrange("(j p) e -> p j e", p=128))
    for q in range(NQ):
        t0 = q * QW
        p.wait("sync", "dve")
        p.wait("sync", "dqg")
        p.wait("sync", "dyt")
        for tl in range(3):
            p.dma("sync", "dqg", p.t["qtr"][:, tl, :], qg_d.ap()[q, 0, tl, :, :])
            p.dma("sync", "dqg", p.t["qti"][:, tl, :], qg_d.ap()[q, 1, tl, :, :])
            p.dma("sync", "dqg", p.t["nqti"][:, tl, :], qg_d.ap()[q, 2, tl, :, :])
            for comp in range(2):
                p.dma("sync", "dyt", p.t["yti"][:, tl, comp, :], yti_d.ap()[q, tl, comp, :, :])
        p.wait("gpsimd", "dve")
        p.wait("gpsimd", "dht")
        p.dma("gpsimd", "dhtl", p.t["Htb"][:, :, :, :, :], ht_d.ap()[q, :, :, :, :, :])
        p.wait("vector", "dhtl")
        p.wait("vector", "dqg")
        p.wait("vector", "dyt")
        if q == 0:
            for tl in range(3):
                for comp in range(2):
                    p.step("vector", "dve", lambda e, tl=tl, comp=comp: e.tensor_copy(
                        p.t["hcarb"][:, tl, comp, :], p.t["hinit"][:, tl, comp, :]))
        for ch in range(NCH):
            for h in range(8):
                tl, b0 = tau_of(h)
                p.wait("tensor", "dve")
                p.wait("tensor", "dhtl")
                p.wait("tensor", "dqg")
                for ss in range(NSC):
                    sg = NSC * ch + ss
                    if sg == 0:
                        hr = p.t["hcarb"][b0:b0 + 32, tl, 0, :]
                        hi = p.t["hcarb"][b0:b0 + 32, tl, 1, :]
                    else:
                        hr = p.t["Htb"][b0:b0 + 32, tl, 0, :, sg - 1]
                        hi = p.t["Htb"][b0:b0 + 32, tl, 1, :, sg - 1]
                    t16 = slice(128 * ch + 16 * ss, 128 * ch + 16 * (ss + 1))
                    p.raw("tensor", lambda e, hr=hr, tl=tl, b0=b0, t16=t16, ss=ss: e.matmul(
                        p.t["pY"][b0:b0 + 32, 16 * ss:16 * (ss + 1)], hr,
                        p.t["qtr"][b0:b0 + 32, tl, t16], start=True, stop=False))
                    p.raw("tensor", lambda e, hi=hi, tl=tl, b0=b0, t16=t16, ss=ss: e.matmul(
                        p.t["pY"][b0:b0 + 32, 16 * ss:16 * (ss + 1)], hi,
                        p.t["nqti"][b0:b0 + 32, tl, t16], start=False, stop=True))
                    p.raw("tensor", lambda e, hr=hr, tl=tl, b0=b0, t16=t16, ss=ss: e.matmul(
                        p.t["pY"][b0:b0 + 32, 128 + 16 * ss:128 + 16 * (ss + 1)], hr,
                        p.t["qti"][b0:b0 + 32, tl, t16], start=True, stop=False))
                    fn = lambda e, hi=hi, tl=tl, b0=b0, t16=t16, ss=ss: e.matmul(
                        p.t["pY"][b0:b0 + 32, 128 + 16 * ss:128 + 16 * (ss + 1)], hi,
                        p.t["qtr"][b0:b0 + 32, tl, t16], start=False, stop=True)
                    if ss == NSC - 1:
                        p.step("tensor", "pe", fn, chain=False)
                    else:
                        p.raw("tensor", fn)
                p.wait("vector", "pe")
                p.step("vector", "dve", lambda e, ch=ch, tl=tl, b0=b0: e.scalar_tensor_tensor(
                    p.t["yti"][b0:b0 + 32, tl, 0, 128 * ch:128 * (ch + 1)], p.t["pY"][b0:b0 + 32, 0:128],
                    1.0, p.t["yti"][b0:b0 + 32, tl, 0, 128 * ch:128 * (ch + 1)], ALU.mult, ALU.add))
                p.step("vector", "dve", lambda e, ch=ch, tl=tl, b0=b0: e.scalar_tensor_tensor(
                    p.t["yti"][b0:b0 + 32, tl, 1, 128 * ch:128 * (ch + 1)], p.t["pY"][b0:b0 + 32, 128:256],
                    1.0, p.t["yti"][b0:b0 + 32, tl, 1, 128 * ch:128 * (ch + 1)], ALU.mult, ALU.add))
        p.wait("vector", "dve")
        for tl in range(3):
            for comp in range(2):
                p.step("vector", "dve", lambda e, tl=tl, comp=comp: e.tensor_copy(
                    p.t["hcarb"][:, tl, comp, :], p.t["Htb"][:, tl, comp, :, NS - 1]))
        for tl in range(3):
            for comp in range(2):
                p.step("vector", "dve", lambda e, tl=tl, comp=comp: e.tensor_copy(
                    p.t["ytb"][:, 3 * comp + tl, :], p.t["yti"][:, tl, comp, :]))
        p.wait("tensor", "dve")
        for et in range(8):
            pp = p.t["pP0"] if et % 2 == 0 else p.t["pP1"]
            p.wait("tensor", "act")
            if et == 0:
                p.wait("tensor", "dw2")
            for j in range(6):
                fn = lambda e, et=et, j=j, pp=pp: e.matmul(
                    pp[:, 0:QW], p.t["wqkva_s"][:, j, 128 * et:128 * (et + 1)],
                    p.t["ytb"][:, j, :], start=(j == 0), stop=(j == 5))
                if j == 5:
                    p.step("tensor", "pe", fn, chain=False)
                else:
                    p.raw("tensor", fn)
            p.wait("scalar", "pe")
            p.step("scalar", "act", lambda e, et=et, pp=pp: e.activation(
                p.t["y2z"][:, et, :], pp[:, 0:QW], AF.Silu, bias=p.t["by_col"][:, et:et + 1]))
        p.wait("tensor", "act")
        for et in range(8):
            pp = p.t["pP0"] if et % 2 == 0 else p.t["pP1"]
            p.wait("tensor", "act")
            p.wait("tensor", "dve")
            for j in range(8):
                fn = lambda e, et=et, j=j, pp=pp: e.matmul(
                    pp[:, 0:QW], p.t["woutT_s"][:, j, 128 * et:128 * (et + 1)],
                    p.t["y2z"][:, j, :], start=(j == 0), stop=(j == 7))
                if j == 7:
                    p.step("tensor", "pe", fn, chain=False)
                else:
                    p.raw("tensor", fn)
            p.wait("vector", "pe")
            p.step("vector", "dve", lambda e, et=et, pp=pp: e.tensor_scalar(
                p.t["outT_bf"][:, et, :], pp[:, 0:QW],
                p.t["bout_col"][:, et:et + 1], None, ALU.add))
        for it in range(QW // 128):
            i = t0 // 128 + it
            p.wait("sync", "dve"); p.wait("sync", "st")
            p.dma("sync", "dx", x_t[:, i % 2, :],
                  x_in.ap().rearrange("(ii pp) d -> pp ii d", pp=128)[:, i, :])
            p.wait("tensor", "act")
            p.wait("tensor", "dve")
            pT = [p.t["pT0"], p.t["pT1"]]
            for h2 in range(2):
                p.wait("tensor", "dve")
                for jj in range(4):
                    j = 4 * h2 + jj
                    fn = lambda e, h2=h2, j=j, jj=jj, it=it: e.transpose(
                        pT[h2][:, 128 * jj:128 * (jj + 1)],
                        p.t["outT_bf"][:, j, 128 * it:128 * (it + 1)], p.t["ident_s"][:, :])
                    if jj == 3:
                        p.step("tensor", "pe", fn, chain=False)
                    else:
                        p.raw("tensor", fn)
                p.wait("vector", "pe")
                p.wait("vector", "dx")
                p.step("vector", "dve",
                       lambda e, i=i, h2=h2: e.tensor_add(
                           x_t[:, i % 2, 512 * h2:512 * (h2 + 1)], pT[h2][:, :],
                           x_t[:, i % 2, 512 * h2:512 * (h2 + 1)]))
            p.wait("sync", "dve")
            p.dma("sync", "st", x_out.ap().rearrange("(ii pp) d -> pp ii d", pp=128)[:, i, :],
                  x_t[:, i % 2, :])
    p.wait("sync", "st")
    return p


def attnb1_in_maps(x_full, inp, prefix="L1_attn_b", ncore=NCORE, tloc=TLOC,
                   cores_per_batch=4):
    QW = min(256, tloc)
    NS = QW // 16
    NSC = NS // (QW // 128)
    ident = host_ident_bf16()
    W_in = np.asarray(inp[f"{prefix}_W_in"], np.float32)
    b_in = np.asarray(inp[f"{prefix}_b_in"], np.float32)
    W_qkva = np.asarray(inp[f"{prefix}_W_qkva"], np.float32)
    b_qkva = np.asarray(inp[f"{prefix}_b_qkva"], np.float32)
    W_y = np.asarray(inp[f"{prefix}_W_y"], np.float32)
    b_y = np.asarray(inp[f"{prefix}_b_y"], np.float32)
    W_out = np.asarray(inp[f"{prefix}_W_out"], np.float32)
    b_out = np.asarray(inp[f"{prefix}_b_out"], np.float32)
    h0_re = np.asarray(inp[f"{prefix}_h0_re"], np.float32)
    h0_im = np.asarray(inp[f"{prefix}_h0_im"], np.float32)
    wq = np.zeros((1024, 3072), np.float32)
    bq = np.zeros((128, 24), np.float32)
    for h in range(8):
        tl, b0 = h // 3, 32 * (h % 3)
        for d in range(32):
            for m in range(8):
                c4, im = m // 2, m % 2
                orig = h * 256 + d * 8 + c4 * 2 + im
                ct = m * 3 + tl
                wq[:, ct * 128 + b0 + d] = W_qkva[orig, :]
                bq[b0 + d, ct] = b_qkva[orig]
    for tl in range(3):
        used = [32 * (h % 3) + d for h in range(8) for d in range(32) if h // 3 == tl]
        pads = sorted(set(range(128)) - set(used))
        for m in (6, 7):
            bq[pads, m * 3 + tl] = 1.0
    wy = np.zeros((768, 1024), np.float32)
    for h in range(8):
        tl, b0 = h // 3, 32 * (h % 3)
        for e in range(32):
            for im in range(2):
                col = h * 64 + e * 2 + im
                wy[384 * im + tl * 128 + b0 + e, :] = W_y[:, col]
    h0p = np.zeros((128, 3, 2, 32), np.float32)
    for h in range(8):
        tl, b0 = h // 3, 32 * (h % 3)
        h0p[b0:b0 + 32, tl, 0, :] = h0_re[h]
        h0p[b0:b0 + 32, tl, 1, :] = h0_im[h]
    mask16 = np.ones((128, QW), np.float32)
    mask16[:, ::16] = 0.0
    s_i = np.arange(128)[:, None]
    t_i = np.arange(128)[None, :]
    triT = ((s_i // 16 == t_i // 16) & (s_i <= t_i)).astype(np.float32)
    maskTD = np.concatenate([triT, triT], axis=1).astype(BF)
    negmaskT = (-triT).astype(BF)
    colv = np.arange(256)[None, :]
    vm = (s_i // 16 == (colv % NSC) + 0 * colv).astype(np.float32)
    virtmask = vm.astype(BF)
    virtmaskn = (-vm).astype(BF)
    i8 = np.zeros((96, 256), np.float32)
    for b0 in (0, 32, 64):
        for d in range(32):
            for sl in range(NSC):
                i8[b0 + d, d * NSC + sl] = 1.0
    ident8v = i8.astype(BF)
    maps = []
    for c in range(ncore):
        crank = c % cores_per_batch
        selcol = np.zeros((128, 4), np.float32)
        selcol[:, :crank] = 1.0
        maps.append({
            "x_in": np.ascontiguousarray(x_full[c * tloc:(c + 1) * tloc]),
            "winT_in": np.ascontiguousarray(W_in.T),
            "b_in_in": b_in.reshape(1, 1024),
            "wqkvaT_in": wq,
            "bqkva_in": bq,
            "wy_in": wy,
            "by_in": np.ascontiguousarray(b_y.reshape(8, 128).T),
            "woutT_in": np.ascontiguousarray(W_out.T),
            "bout_in": np.ascontiguousarray(b_out.reshape(8, 128).T),
            "h0_in": h0p,
            "selcol_in": selcol,
            "flag0_in": np.full((128, 1), 1.0 if crank == 0 else 0.0, np.float32),
            "mask16_in": mask16,
            "maskTD_in": maskTD,
            "negmaskT_in": negmaskT,
            "virtmask_in": virtmask,
            "virtmaskn_in": virtmaskn,
            "ident8v_in": ident8v,
            "ident_in": ident,
        })
    return maps


def host_ident_bf16():
    return np.eye(128, dtype=np.float32).astype(BF)


def ffn_in_maps(x_full, inp, prefix="L0_ffn", ncore=NCORE, tloc=TLOC):
    ident = host_ident_bf16()
    maps = []
    for c in range(ncore):
        xs = np.ascontiguousarray(x_full[c * tloc:(c + 1) * tloc])
        maps.append({
            "x_in": xs,
            "wT": np.ascontiguousarray(inp[f"{prefix}_W"].T),
            "b_in": np.ascontiguousarray(inp[f"{prefix}_b"].reshape(1, 1024)),
            "ldT": np.ascontiguousarray(inp[f"{prefix}_lora_down"].T),
            "down_aux": np.ascontiguousarray(inp[f"{prefix}_down_aux"]),
            "luT": np.ascontiguousarray(inp[f"{prefix}_lora_up"].T),
            "up_auxT": np.ascontiguousarray(inp[f"{prefix}_up_aux"].T),
            "ident_in": ident,
        })
    return maps


def rwkv_in_maps(x_full, inp, prefix="L0_attn_a", ncore=NCORE, tloc=TLOC,
                 cores_per_batch=4):
    ident = host_ident_bf16()
    triuT = np.triu(np.ones((128, 128), np.float32))
    ln_w = np.asarray(inp[f"{prefix}_ln_w"], np.float32)
    maps = []
    for c in range(ncore):
        crank = c % cores_per_batch
        iota = (np.arange(tloc, dtype=np.float32) + crank * tloc).reshape(1, tloc)
        sel = (np.arange(cores_per_batch) < crank).astype(np.float32).reshape(-1, 1)
        maps.append({
            "x_in": np.ascontiguousarray(x_full[c * tloc:(c + 1) * tloc]),
            "wrT": np.ascontiguousarray(np.asarray(inp[f"{prefix}_Wr"]).T),
            "wkT": np.ascontiguousarray(np.asarray(inp[f"{prefix}_Wk"]).T),
            "wvT": np.ascontiguousarray(np.asarray(inp[f"{prefix}_Wv"]).T),
            "woT": np.ascontiguousarray(np.asarray(inp[f"{prefix}_Wo"]).T),
            "decay_in": np.asarray(inp[f"{prefix}_decay"], np.float32).reshape(1, 1024),
            "lnw_col_in": np.ascontiguousarray(ln_w.reshape(8, 128).T),
            "iota_in": iota,
            "sel_in": sel,
            "triuT_in": triuT,
            "ident_in": ident,
        })
    return maps


def host_fft_consts():
    e = np.arange(1024)[:, None]
    f = np.arange(FPAD)[None, :]
    ang = 2.0 * np.pi * e * f / 1024.0
    Fre = np.cos(ang); Fim = -np.sin(ang)
    Fre[:, 513:] = 0.0; Fim[:, 513:] = 0.0
    f_in = np.concatenate([Fre, Fim], axis=1).astype(np.float32)
    gr = np.fft.irfft(np.eye(513), n=1024, axis=-1)
    gi = np.fft.irfft(1j * np.eye(513), n=1024, axis=-1)
    G = np.zeros((2 * FPAD, 1024), np.float64)
    G[0:513] = gr; G[FPAD:FPAD + 513] = gi
    ginvT = np.ascontiguousarray((G.T * 0.1).astype(np.float32))
    return f_in, ginvT


def ffta_a_in_maps(x_full, inp, prefix="L1_attn_a", ncore=NCORE, tloc=TLOC):
    f_in, ginvT = host_fft_consts()
    ident = host_ident_bf16()
    maps = []
    for c in range(ncore):
        maps.append({
            "x_in": np.ascontiguousarray(x_full[c * tloc:(c + 1) * tloc]),
            "wk_in": np.ascontiguousarray(np.asarray(inp[f"{prefix}_Wk"], np.float32)),
            "wq_in": np.ascontiguousarray(np.asarray(inp[f"{prefix}_Wq"], np.float32)),
            "wv_in": np.ascontiguousarray(np.asarray(inp[f"{prefix}_Wv"], np.float32)),
            "f_in": f_in,
            "ginvT_in": ginvT,
            "woT_in": np.ascontiguousarray(np.asarray(inp[f"{prefix}_Wo"], np.float32).T),
            "ident_in": ident,
        })
    return maps


def ffta_b_in_maps(x_full, spectra, inp, prefix="L1_attn_a", ncore=NCORE, tloc=TLOC,
                   cores_per_batch=4):
    ident = host_ident_bf16()
    perms = np.asarray(inp[f"{prefix}_perms"], np.int32)
    perms_pad = np.pad(perms, ((0, 0), (0, FPAD - perms.shape[1])), constant_values=513)
    maps = []
    for c in range(ncore):
        crank = c % cores_per_batch
        sel = (np.arange(cores_per_batch) < crank).astype(np.float32).reshape(-1, 1)
        m = {
            "x_in": np.ascontiguousarray(x_full[c * tloc:(c + 1) * tloc]),
            "perms_in": perms_pad,
            "sel_in": sel,
            "ident_in": ident,
        }
        if spectra is not None:
            for k2 in ["fkre_d", "fkim_d", "fqre_d", "fqim_d", "fv_d", "h_d"]:
                m[k2] = spectra[c][k2]
        maps.append(m)
    return maps


SPECTRA_KEYS = ["fkre_d", "fkim_d", "fqre_d", "fqim_d", "fv_d", "h_d"]


def build_merged(tloc=TLOC, groups=None, tfull=None):
    par = Prog("mk", tloc)
    rng = par.nc.bump_sbuf(206 * 1024)
    assert rng is not None
    par.ARENA_BASE = rng[0]
    par.ARENA_TOP = rng[1]
    par.arena_reset()
    groups = groups or GROUPS
    state = {"prev": None}

    def seg(tag, buildfn, aliases=None, out_kind="Internal"):
        if state["prev"] is not None:
            par.barrier()
            state["prev"].seg_ctx.close()
        par.arena_reset()
        sp = SegmentProg(par, tag, out_kind=out_kind, dram_alias=aliases or {})
        buildfn(sp)
        state["prev"] = sp
        return sp

    s1 = seg("rwkv", lambda sp: build_rwkv(tloc, groups=groups, p=sp))
    s2 = seg("ffn1", lambda sp: build_ffn(tloc, p=sp),
             aliases={"x_in": s1.t["x_out"]})
    s3 = seg("attnb", lambda sp: build_attnb(tloc, groups=groups, tfull=tfull, p=sp),
             aliases={"x_in": s2.t["x_out"]})
    s4 = seg("ffn2", lambda sp: build_ffn(tloc, p=sp),
             aliases={"x_in": s3.t["x_out"]})
    s5 = seg("ffta_a", lambda sp: build_ffta_a(tloc, p=sp),
             aliases={"x_in": s4.t["x_out"]})
    s6 = seg("ffta_b", lambda sp: build_ffta_b(tloc, groups=groups, p=sp),
             aliases=dict({"x_in": s4.t["x_out"]},
                          **{k: s5.t[k] for k in SPECTRA_KEYS}))
    s7 = seg("moe1", lambda sp: build_moe(tloc, p=sp),
             aliases={"x_in": s6.t["x_out"]})
    s8 = seg("attnb1", lambda sp: build_attnb1(tloc, groups=groups, p=sp),
             aliases={"x_in": s7.t["x_out"]})
    seg("moe2", lambda sp: build_moe(tloc, p=sp),
        aliases={"x_in": s8.t["x_out"]}, out_kind="ExternalOutput")
    state["prev"].seg_ctx.close()
    return par, "moe2_x_out"


def merged_in_maps(x, inp, ncore=NCORE, tloc=TLOC, cores_per_batch=4,
                   tfull=None, nheads_total=16):
    per_seg = {
        "rwkv": rwkv_in_maps(x, inp, ncore=ncore, tloc=tloc,
                             cores_per_batch=cores_per_batch),
        "ffn1": ffn_in_maps(x, inp, "L0_ffn", ncore=ncore, tloc=tloc),
        "attnb": attnb_in_maps(x, inp, ncore=ncore, tloc=tloc,
                               cores_per_batch=cores_per_batch, tfull=tfull,
                               nheads_total=nheads_total),
        "ffn2": ffn_in_maps(x, inp, "L0_ffn", ncore=ncore, tloc=tloc),
        "ffta_a": ffta_a_in_maps(x, inp, ncore=ncore, tloc=tloc),
        "ffta_b": ffta_b_in_maps(x, None, inp, ncore=ncore, tloc=tloc,
                                 cores_per_batch=cores_per_batch),
        "moe1": moe_in_maps(x, inp, ncore=ncore, tloc=tloc),
        "attnb1": attnb1_in_maps(x, inp, ncore=ncore, tloc=tloc,
                                 cores_per_batch=cores_per_batch),
        "moe2": moe_in_maps(x, inp, ncore=ncore, tloc=tloc),
    }
    maps = [dict() for _ in range(ncore)]
    for tag, ms in per_seg.items():
        for c in range(ncore):
            for k, v in ms[c].items():
                if k == "x_in" and tag != "rwkv":
                    continue
                maps[c][f"{tag}_{k}"] = v
    return maps


def run_prog(p, in_maps):
    nc = p.build()
    res = run_bass_kernel_spmd(nc, in_maps, core_ids=list(range(NCORE)))
    return res


def kernel(**inputs):
    inp = {k: np.asarray(v) for k, v in inputs.items()}
    x = np.ascontiguousarray(inp["L0_attn_a_x"].reshape(B * T, D).astype(np.float32))
    par, out_key = build_merged()
    nc = par.build()
    res = run_bass_kernel_spmd(nc, merged_in_maps(x, inp),
                               core_ids=list(range(NCORE)))
    x = np.concatenate([res.results[c][out_key] for c in range(NCORE)], axis=0)
    return x.reshape(B, T, D)


def kernel_multilaunch(**inputs):
    inp = {k: np.asarray(v) for k, v in inputs.items()}
    x = np.ascontiguousarray(inp["L0_attn_a_x"].reshape(B * T, D).astype(np.float32))
    cores = list(range(NCORE))

    def gather(res):
        return np.concatenate([res.results[c]["x_out"] for c in range(NCORE)], axis=0)

    x = gather(run_prog(build_rwkv(), rwkv_in_maps(x, inp)))
    nc_ffn = build_ffn().build()
    x = gather(run_bass_kernel_spmd(nc_ffn, ffn_in_maps(x, inp, "L0_ffn"), core_ids=cores))
    x = gather(run_prog(build_attnb(), attnb_in_maps(x, inp)))
    x = gather(run_bass_kernel_spmd(nc_ffn, ffn_in_maps(x, inp, "L0_ffn"), core_ids=cores))
    ra = run_prog(build_ffta_a(), ffta_a_in_maps(x, inp))
    spectra = [{k2: ra.results[c][k2] for k2 in
                ["fkre_d", "fkim_d", "fqre_d", "fqim_d", "fv_d", "h_d"]} for c in range(NCORE)]
    x = gather(run_prog(build_ffta_b(), ffta_b_in_maps(x, spectra, inp)))
    nc_moe = build_moe().build()
    x = gather(run_bass_kernel_spmd(nc_moe, moe_in_maps(x, inp), core_ids=cores))
    x = gather(run_prog(build_attnb1(), attnb1_in_maps(x, inp)))
    x = gather(run_bass_kernel_spmd(nc_moe, moe_in_maps(x, inp), core_ids=cores))
    return x.reshape(B, T, D)
